# version 05
# baseline (speedup 1.0000x reference)
import numpy as np
from contextlib import ExitStack

import concourse.bass as bass
import concourse.mybir as mybir
from concourse.alu_op_type import AluOpType as OP
from concourse.bass_utils import run_bass_kernel_spmd

F32 = mybir.dt.float32
BF16 = mybir.dt.bfloat16
I32 = mybir.dt.int32
AF = mybir.ActivationFunctionType
AX = mybir.AxisListType

D = 1024
T = 4096
NCORES = 8
PI = float(np.pi)
TWO_PI = float(2.0 * np.pi)


ENGINES = ("sync", "tensor", "vector", "scalar", "gpsimd")


class PB:
    def __init__(self, nc, es):
        self.nc = nc
        self.es = es
        self.sem = {e: es.enter_context(nc.semaphore("sem_" + e)) for e in ENGINES}
        self.cnt = {e: 0 for e in ENGINES}
        self.seen = {c: {p: 0 for p in ENGINES} for c in ENGINES}
        self.q = {e: [] for e in ENGINES}
        self.state = {}
        self._name_i = 0

    def sbuf(self, name, shape, dtype):
        t = self.es.enter_context(self.nc.sbuf_tensor(name, list(shape), dtype))
        return t

    def psum(self, name, shape, dtype=F32):
        return self.es.enter_context(self.nc.psum_tensor(name, list(shape), dtype))

    def _waits(self, eng, deps):
        for d in deps:
            if d is None:
                continue
            p, v = d
            if self.seen[eng][p] >= v:
                continue
            self.seen[eng][p] = v
            sem = self.sem[p]
            self.q[eng].append(lambda e, sem=sem, v=v: e.wait_ge(sem, v))

    def comp(self, eng, fn, deps=()):
        self._waits(eng, deps)
        sem = self.sem[eng]

        def run(e, fn=fn, sem=sem):
            last = fn(e)
            last.then_inc(sem, 1)

        self.q[eng].append(run)
        self.cnt[eng] += 1
        return (eng, self.cnt[eng])

    def dma(self, eng, fn, n=1, deps=()):
        self._waits(eng, deps)
        sem = self.sem[eng]
        self.q[eng].append(lambda e, fn=fn, sem=sem: fn(e, sem))
        self.cnt[eng] += 16 * n
        return (eng, self.cnt[eng])

    def step(self, eng, fn, inc, deps=()):
        self._waits(eng, deps)
        sem = self.sem[eng]
        base = self.cnt[eng]
        self.q[eng].append(lambda e, fn=fn, sem=sem, base=base: fn(e, sem, base))
        self.cnt[eng] += inc
        return (eng, self.cnt[eng])

    def raw(self, eng, fn, deps=()):
        self._waits(eng, deps)
        self.q[eng].append(fn)
        return None

    def barrier(self):
        toks = self.all_toks()
        for e in ENGINES:
            self._waits(e, toks)

    def tok(self, eng):
        return (eng, self.cnt[eng])

    def all_toks(self):
        return [(e, self.cnt[e]) for e in ENGINES if self.cnt[e] > 0]

    def finish(self):
        def fn(e):
            for eng in ENGINES:
                if self.cnt[eng] > 0:
                    e.wait_ge(self.sem[eng], self.cnt[eng])
        self.q["sync"].append(fn)

    def finalize(self, block):
        for e in ENGINES:
            if self.q[e]:
                def body(obj, lst=tuple(self.q[e])):
                    for f in lst:
                        f(obj)
                getattr(block, e)(body)



class Consts:

    def __init__(self, pb):
        self.pb = pb
        nc = pb.nc
        self.identf = pb.sbuf("c_identf", [128, 128], F32)
        self.identb = pb.sbuf("c_identb", [128, 128], BF16)
        self.onesb_col = pb.sbuf("c_onesb_col", [128, 1], BF16)
        self.onesf_col = pb.sbuf("c_onesf_col", [128, 1], F32)
        self.onesb_row = pb.sbuf("c_onesb_row", [1, 128], BF16)
        self.onesf_row = pb.sbuf("c_onesf_row", [1, 128], F32)
        self.ones_tile = pb.sbuf("c_ones_tile", [128, 512], F32)
        self._iota_sc = pb.sbuf("c_iota_sc", [128, 128], F32)
        self._cvals = {}
        self._cv_tile = pb.sbuf("c_cvals", [128, 16], F32)
        self._cv_used = 0
        self.ready = None

    def cval(self, value, nP=128):
        if value not in self._cvals:
            assert self._cv_used < 16
            self._cvals[value] = self._cv_used
            self._cv_used += 1
        i = self._cvals[value]
        return self._cv_tile[0:nP, i : i + 1]

    def build(self, extra_cvals=()):
        pb = self.pb
        for v in extra_cvals:
            self.cval(v)
        cvals = dict(self._cvals)

        def g_fn(g):
            last = g.iota(out=self._iota_sc[:], pattern=[[1, 128]], base=0,
                          channel_multiplier=-1, allow_small_or_imprecise_dtypes=True)
            last = g.memset(self.ones_tile[:], 1.0)
            for v, idx in cvals.items():
                last = g.memset(self._cv_tile[:, idx:idx + 1], float(v))
            return last

        t_g = pb.comp("gpsimd", g_fn)

        def v_fn(v):
            v.tensor_scalar(out=self.identf[:], in0=self._iota_sc[:], scalar1=0.0,
                            scalar2=None, op0=OP.is_equal, op1=OP.bypass)
            v.tensor_copy(out=self.identb[:], in_=self.identf[:])
            v.memset(self.onesb_col[:], 1.0)
            v.memset(self.onesf_col[:], 1.0)
            v.memset(self.onesb_row[:], 1.0)
            return v.memset(self.onesf_row[:], 1.0)

        self.ready = pb.comp("vector", v_fn, deps=[t_g])
        return self.ready



def stream_matmul(pb, w_dram, nK, ne, xb, C, wbufs, pss, evac, rhs_deps=(), e0=0):
    nw = len(wbufs)
    st_w = pb.state.setdefault(("wbuf", id(wbufs[0])), [None] * nw)
    st_p = pb.state.setdefault(("psum", id(pss[0])), [None, None])
    n_dma_per_ld = (nK + 15) // 16
    out_toks = []
    for i in range(ne):
        ei = e0 + i
        pw = i % nw

        def ldfn(e, sem, ei=ei, pw=pw):
            for g0 in range(0, nK, 16):
                g1 = min(g0 + 16, nK)
                src = w_dram[g0 * 128:g1 * 128, ei * 128:(ei + 1) * 128].rearrange(
                    "(kt p) e -> p kt e", p=128)
                dst = wbufs[pw][:, g0 * 128:g1 * 128].rearrange("p (kt e) -> p kt e", kt=g1 - g0)
                e.dma_start(out=dst, in_=src).then_inc(sem, 16)

        ld_tok = pb.dma("gpsimd", ldfn, n=n_dma_per_ld, deps=[st_w[pw]])

        def mmfn(t, pw=pw, pp=i % 2):
            last = None
            for k in range(nK):
                last = t.matmul(out=pss[pp][:, :C], lhsT=wbufs[pw][:, k * 128:(k + 1) * 128],
                                rhs=xb[k], start=(k == 0), stop=(k == nK - 1))
            return last

        st_w[pw] = pb.comp("tensor", mmfn, deps=[ld_tok, st_p[i % 2]] + list(rhs_deps))
        st_p[i % 2] = evac(i, pss[i % 2], st_w[pw])
        out_toks.append(st_p[i % 2])
    return out_toks


def emit_ln(pb, cst, sc, xs, C, nK, apply_fn, eps=1e-5, xs_dep=None, xs_are_bf16=False,
            denom=None):
    tb, tq = sc["tb"], sc["tq"]
    ps_s1, ps_s2 = sc["ps_s1"], sc["ps_s2"]
    psb0, psb1 = sc["psb0"], sc["psb1"]
    rows = sc["rows"]
    st = pb.state.setdefault(("lnsc", id(ps_s1)), {"comb": None, "applies": []})
    cast_tok = [None, None]
    sq_tok = [None, None]
    mm_prev = None
    for k in range(nK):
        p = k % 2
        war = [mm_prev] if mm_prev is not None else [st["comb"]] + st["applies"]
        if xs_are_bf16:
            cast_tok[p] = xs_dep
        else:
            cast_tok[p] = pb.comp("vector", lambda v, k=k, p=p: v.tensor_copy(out=tb[p][:, :C], in_=xs[k]),
                                  deps=war + [xs_dep])
        sq_tok[p] = pb.comp("scalar", lambda s, k=k, p=p: s.activation(out=tq[p][:, :C], in_=xs[k], func=AF.Square),
                            deps=war + [xs_dep])

        def mmfn(t, k=k, p=p):
            rhs1 = xs[k] if xs_are_bf16 else tb[p][:, :C]
            t.matmul(out=ps_s1[0:1, :C], lhsT=cst.onesb_col[:], rhs=rhs1,
                     start=(k == 0), stop=(k == nK - 1))
            return t.matmul(out=ps_s2[0:1, :C], lhsT=cst.onesb_col[:], rhs=tq[p][:, :C],
                            start=(k == 0), stop=(k == nK - 1))

        mm_prev = pb.comp("tensor", mmfn, deps=[cast_tok[p], sq_tok[p], st["comb"]])

    inv = 1.0 / (denom if denom is not None else nK * 128)

    def comb_v(v):
        v.tensor_scalar(out=rows[0][:, :C], in0=ps_s1[0:1, :C], scalar1=inv, scalar2=None,
                        op0=OP.mult, op1=OP.bypass)
        v.tensor_scalar(out=rows[1][:, :C], in0=ps_s2[0:1, :C], scalar1=inv, scalar2=None,
                        op0=OP.mult, op1=OP.bypass)
        v.tensor_tensor(out=rows[2][:, :C], in0=rows[0][:, :C], in1=rows[0][:, :C], op=OP.mult)
        return v.tensor_tensor(out=rows[2][:, :C], in0=rows[1][:, :C], in1=rows[2][:, :C], op=OP.subtract)

    t_comb = pb.comp("vector", comb_v, deps=[mm_prev, st.get("bcast")])

    def comb_s(s):
        s.activation(out=rows[3][:, :C], in_=rows[2][:, :C], func=AF.Ln, bias=cst.cval(eps, nP=1))
        return s.activation(out=rows[3][:, :C], in_=rows[3][:, :C], func=AF.Exp, scale=-0.5)

    t_rstd = pb.comp("scalar", comb_s, deps=[t_comb])

    def bcast(t):
        t.matmul(out=psb0[:, :C], lhsT=cst.onesf_row[:], rhs=rows[0][:, :C], start=True, stop=True)
        return t.matmul(out=psb1[:, :C], lhsT=cst.onesf_row[:], rhs=rows[3][:, :C], start=True, stop=True)

    t_bc = pb.comp("tensor", bcast, deps=[t_rstd] + st["applies"])

    toks = [apply_fn(k, psb0, psb1, t_bc) for k in range(nK)]
    st["comb"] = t_comb
    st["bcast"] = t_bc
    st["applies"] = list(toks)
    return toks, t_bc



def emit_l0_attn_a(pb, cst, W, xin, xout, dbg=None, Tlim=T):
    es = pb.es
    C = 512
    NCH = Tlim // C
    sb = lambda n, s, d=F32: pb.sbuf("l0aa_" + n, s, d)

    Xc = [sb(f"x{k}", [128, C]) for k in range(8)]
    XN = [sb(f"xn{k}", [128, C]) for k in range(8)]
    XNb = [sb(f"xnb{k}", [128, C], BF16) for k in range(8)]
    ctx = [sb(f"ctx{k}", [128, C], BF16) for k in range(32)]
    xh = [sb(f"xh{k}", [128, C], BF16) for k in range(32)]
    h1g = [sb(f"h1g{k}", [128, C], BF16) for k in range(16)]
    posc = sb("posc", [128, C])
    posinv = sb("posinv", [128, C])
    pos = sb("pos", [128, C])
    om = sb("om", [128, C])
    phi = sb("phi", [128, C])
    kf = sb("kf", [128, C])
    ki = sb("ki", [128, C], I32)
    phr = sb("phr", [128, C])
    sinp = sb("sinp", [128, C])
    cosp = sb("cosp", [128, C])
    cmr = sb("cmr", [128, C])
    cmi = sb("cmi", [128, C])
    memr = sb("memr", [128, C])
    memi = sb("memi", [128, C])
    tA = sb("tA", [128, C])
    tB = sb("tB", [128, C])
    carry = sb("carry", [128, 24])
    outb = [sb(f"outb{p}", [128, C]) for p in range(2)]
    t4 = [sb(f"t4_{p}", [128, C]) for p in range(2)]
    wbufs = [sb(f"wbuf{p}", [128, 32 * 128], BF16) for p in range(3)]
    b_om8 = sb("b_om8", [128, 8]); els = sb("els", [128, 8]); b2om = sb("b2om", [128, 8])
    g32 = sb("g32", [128, 32]); b32 = sb("b32", [128, 32])
    b1_16 = sb("b1_16", [128, 16]); b2_8 = sb("b2_8", [128, 8])
    lnsc = {
        "tb": [sb(f"lntb{p}", [128, C], BF16) for p in range(2)],
        "tq": [sb(f"lntq{p}", [128, C], BF16) for p in range(2)],
        "rows": [sb(f"lnrow{i}", [1, C]) for i in range(4)],
        "ps_s1": pb.psum("l0aa_ps_s1", [128, C]),
        "ps_s2": pb.psum("l0aa_ps_s2", [128, C]),
        "psb0": pb.psum("l0aa_psb0", [128, C]),
        "psb1": pb.psum("l0aa_psb1", [128, C]),
    }
    pss = [pb.psum("l0aa_pp0", [128, C]), pb.psum("l0aa_pp1", [128, C])]

    def ldparams(e, sem):
        for name, dst in [("b_om8", b_om8), ("ls8", els), ("g32", g32), ("b32", b32),
                          ("b1_16", b1_16), ("b2_8", b2_8)]:
            e.dma_start(out=dst[:], in_=W[name]).then_inc(sem, 16)

    t_par = pb.dma("sync", ldparams, n=6, deps=[cst.ready])
    t_els = pb.comp("scalar", lambda s: s.activation(out=els[:], in_=els[:], func=AF.Exp), deps=[t_par])
    t_b2om = pb.comp("vector", lambda v: v.tensor_tensor(out=b2om[:], in0=b_om8[:], in1=els[:], op=OP.mult),
                     deps=[t_els])

    last_resadd = [None] * 8
    last_xn_use = [None] * 8
    prev_cascade = [t_b2om]
    dma_out_tok = [None, None]

    for ci in range(NCH):
        t0 = ci * C
        def ldx(e, sem, t0=t0):
            for k in range(8):
                e.dma_start(out=Xc[k][:], in_=xin[k * 128:(k + 1) * 128, t0:t0 + C]).then_inc(sem, 16)
        t_x = pb.dma("sync", ldx, n=8, deps=[cst.ready] + last_resadd)

        t_iota = pb.comp("gpsimd", lambda g, t0=t0: g.iota(
            out=pos[:], pattern=[[1, C]], base=t0 + 1, channel_multiplier=0,
            allow_small_or_imprecise_dtypes=True), deps=prev_cascade)

        def posfn(s):
            s.activation(out=pos[:], in_=pos[:], func=AF.Ln)
            s.activation(out=posc[:], in_=pos[:], func=AF.Exp, scale=-0.5)
            return s.activation(out=posinv[:], in_=pos[:], func=AF.Exp, scale=-1.0)
        t_pos = pb.comp("scalar", posfn, deps=[t_iota])

        def ln1_apply(k, psb0, psb1, dep):
            def vfn(v, k=k):
                v.tensor_tensor(out=tA[:], in0=Xc[k][:], in1=psb0[:, :C], op=OP.subtract)
                v.tensor_tensor(out=XN[k][:], in0=tA[:], in1=psb1[:, :C], op=OP.mult)
                return v.tensor_copy(out=XNb[k][:], in_=XN[k][:])
            return pb.comp("vector", vfn,
                           deps=[dep, prev_cascade[0], last_xn_use[k]])
        ln1_toks, _ = emit_ln(pb, cst, lnsc, [x[:] for x in Xc], C, 8, ln1_apply, xs_dep=t_x)

        def stageA_evac(i, ps, mmtok, ci=ci):
            e = i
            dep0 = [mmtok, t_pos, t_b2om] + prev_cascade
            t_s1 = pb.comp("vector", lambda v, e=e: v.tensor_scalar(
                out=om[:], in0=ps[:, :C], scalar1=els[:, e:e + 1], scalar2=b2om[:, e:e + 1],
                op0=OP.mult, op1=OP.add), deps=dep0)

            def v1(v, sem, base, e=e, ci=ci):
                v.tensor_tensor(out=om[:], in0=om[:], in1=posc[:], op=OP.mult)
                init = 0.0 if ci == 0 else carry[:, e:e + 1]
                v.tensor_tensor_scan(out=phi[:], data0=cst.ones_tile[:, :C], data1=om[:],
                                     initial=init, op0=OP.mult, op1=OP.add).then_inc(sem, 1)
                v.wait_ge(sem, base + 1)
                v.tensor_copy(out=carry[:, e:e + 1], in_=phi[:, C - 1:C])
                v.tensor_scalar(out=kf[:], in0=phi[:], scalar1=1.0 / TWO_PI, scalar2=None,
                                op0=OP.mult, op1=OP.bypass)
                v.tensor_copy(out=ki[:], in_=kf[:])
                v.tensor_copy(out=kf[:], in_=ki[:])
                v.scalar_tensor_tensor(out=phr[:], in0=kf[:], scalar=-TWO_PI, in1=phi[:],
                                       op0=OP.mult, op1=OP.add).then_inc(sem, 1)
            t_v1 = pb.step("vector", v1, 2, deps=[t_s1])
            t_s2 = pb.comp("scalar", lambda s: s.activation(out=sinp[:], in_=phr[:], func=AF.Sin),
                           deps=[t_v1])
            t_v2 = pb.comp("scalar", lambda s: s.activation(out=tA[:], in_=phr[:], func=AF.Abs),
                           deps=[t_v1])
            t_s3 = pb.comp("scalar", lambda s: s.activation(
                out=cosp[:], in_=tA[:], func=AF.Sin, scale=-1.0, bias=cst.cval(PI / 2)), deps=[t_v2])

            def v3(v, sem, base, e=e, ci=ci):
                v.tensor_tensor(out=cmr[:], in0=XN[e][:], in1=cosp[:], op=OP.mult)
                v.tensor_tensor(out=cmi[:], in0=XN[e][:], in1=sinp[:], op=OP.mult)
                v.tensor_copy(out=ctx[e][:], in_=cmr[:])
                v.tensor_copy(out=ctx[8 + e][:], in_=cmi[:])
                initr = 0.0 if ci == 0 else carry[:, 8 + e:9 + e]
                v.tensor_tensor_scan(out=tA[:], data0=cst.ones_tile[:, :C], data1=cmr[:],
                                     initial=initr, op0=OP.mult, op1=OP.add).then_inc(sem, 1)
                v.wait_ge(sem, base + 1)
                v.tensor_copy(out=carry[:, 8 + e:9 + e], in_=tA[:, C - 1:C])
                v.tensor_tensor(out=memr[:], in0=tA[:], in1=posinv[:], op=OP.mult)
                initi = 0.0 if ci == 0 else carry[:, 16 + e:17 + e]
                v.tensor_tensor_scan(out=tA[:], data0=cst.ones_tile[:, :C], data1=cmi[:],
                                     initial=initi, op0=OP.mult, op1=OP.add).then_inc(sem, 1)
                v.wait_ge(sem, base + 2)
                v.tensor_copy(out=carry[:, 16 + e:17 + e], in_=tA[:, C - 1:C])
                v.tensor_tensor(out=memi[:], in0=tA[:], in1=posinv[:], op=OP.mult)
                v.tensor_tensor(out=tA[:], in0=memr[:], in1=cosp[:], op=OP.mult)
                v.tensor_tensor(out=tB[:], in0=memi[:], in1=sinp[:], op=OP.mult)
                v.tensor_tensor(out=ctx[16 + e][:], in0=tA[:], in1=tB[:], op=OP.add)
                v.tensor_tensor(out=tA[:], in0=memi[:], in1=cosp[:], op=OP.mult)
                v.tensor_tensor(out=tB[:], in0=memr[:], in1=sinp[:], op=OP.mult)
                v.tensor_tensor(out=ctx[24 + e][:], in0=tA[:], in1=tB[:], op=OP.subtract).then_inc(sem, 1)
            t_v3 = pb.step("vector", v3, 3, deps=[t_s3, ln1_toks[e]])
            last_xn_use[e] = t_v3
            prev_cascade[:] = [t_v3]
            return t_v3

        ctx_toks = stream_matmul(pb, W["womT"], 8, 8, [x[:] for x in XNb], C, wbufs, pss,
                                 stageA_evac, rhs_deps=ln1_toks)

        prev_apply = [None]

        def ln2_apply_chain(k, psb0, psb1, dep):
            tok = ln2_apply_inner(k, psb0, psb1, [dep] + ([prev_apply[0]] if prev_apply[0] else []))
            prev_apply[0] = tok
            return tok

        def ln2_apply_inner(k, psb0, psb1, deps):
            def vfn(v, k=k):
                v.tensor_tensor(out=tA[:], in0=ctx[k][:], in1=psb0[:, :C], op=OP.subtract)
                v.tensor_tensor(out=tB[:], in0=tA[:], in1=psb1[:, :C], op=OP.mult)
                return v.tensor_scalar(out=xh[k][:], in0=tB[:], scalar1=g32[:, k:k + 1],
                                       scalar2=b32[:, k:k + 1], op0=OP.mult, op1=OP.add)
            return pb.comp("vector", vfn, deps=list(deps) + [t_par])

        ln2_toks, _ = emit_ln(pb, cst, lnsc, [c[:] for c in ctx], C, 32, ln2_apply_chain,
                              xs_dep=ctx_toks[-1], xs_are_bf16=True)

        def w1_evac(i, ps, mmtok):
            return pb.comp("scalar", lambda s, i=i: s.activation(
                out=h1g[i][:], in_=ps[:, :C], func=AF.Gelu, bias=b1_16[:, i:i + 1]),
                deps=[mmtok, t_par])
        w1_toks = stream_matmul(pb, W["w1T"], 32, 16, [x[:] for x in xh], C, wbufs, pss,
                                w1_evac, rhs_deps=ln2_toks)

        def w2_evac(i, ps, mmtok, ci=ci):
            p = i % 2

            def vfn(v, i=i, p=p):
                v.tensor_scalar(out=t4[p][:], in0=ps[:, :C], scalar1=b2_8[:, i:i + 1],
                                scalar2=None, op0=OP.add, op1=OP.bypass)
                v.tensor_tensor(out=t4[p][:], in0=t4[p][:], in1=XN[i][:], op=OP.add)
                return v.tensor_tensor(out=outb[p][:], in0=t4[p][:], in1=Xc[i][:], op=OP.add)
            t_v = pb.comp("vector", vfn, deps=[mmtok, dma_out_tok[p], t_par])
            last_resadd[i] = t_v
            last_xn_use[i] = t_v

            def st(e, sem, i=i, p=p, t0=t0):
                e.dma_start(out=xout[i * 128:(i + 1) * 128, t0:t0 + C], in_=outb[p][:]).then_inc(sem, 16)
            t_d = pb.dma("sync", st, n=1, deps=[t_v])
            dma_out_tok[p] = t_d
            return t_v
        w2_toks = stream_matmul(pb, W["w2T"], 16, 8, [x[:] for x in h1g], C, wbufs, pss,
                                w2_evac, rhs_deps=w1_toks)

    if dbg is not None:
        named = {"xn": XN, "xnb": XNb, "ctx": ctx, "xh": xh, "h1g": h1g,
                 "om": [om], "phi": [phi], "sinp": [sinp], "cosp": [cosp],
                 "cmr": [cmr], "memr": [memr], "carry": [carry]}

        def dumps(e, sem):
            nd = 0
            for key, ap in dbg.items():
                tiles = named[key]
                for i, tl in enumerate(tiles):
                    e.dma_start(out=ap[i * 128:(i + 1) * 128, :], in_=tl[:]).then_inc(sem, 16)
                    nd += 1
            return nd

        ndbg = sum(len(named[k]) for k in dbg)
        pb.dma("gpsimd", lambda e, sem: dumps(e, sem), n=ndbg, deps=pb.all_toks())

    return pb.tok("sync")


def host_prep_l0_attn_a(inp):
    r = {}
    r["womT"] = np.ascontiguousarray(inp["L0_attn_a_W_omega"].T)
    r["b_om8"] = np.ascontiguousarray(inp["L0_attn_a_b_omega"].reshape(8, 128).T)
    r["ls8"] = np.ascontiguousarray(inp["L0_attn_a_log_scale"].reshape(8, 128).T)
    r["g32"] = np.ascontiguousarray(inp["L0_attn_a_ln_gamma"].reshape(32, 128).T)
    r["b32"] = np.ascontiguousarray(inp["L0_attn_a_ln_beta"].reshape(32, 128).T)
    r["w1T"] = np.ascontiguousarray(inp["L0_attn_a_W1"].T)
    r["b1_16"] = np.ascontiguousarray(inp["L0_attn_a_b1"].reshape(16, 128).T)
    r["w2T"] = np.ascontiguousarray(inp["L0_attn_a_W2"].T)
    r["b2_8"] = np.ascontiguousarray(inp["L0_attn_a_b2"].reshape(8, 128).T)
    return r


L0AA_WSPECS = [("womT", [1024, 1024]), ("b_om8", [128, 8]), ("ls8", [128, 8]),
               ("g32", [128, 32]), ("b32", [128, 32]), ("w1T", [4096, 2048]),
               ("b1_16", [128, 16]), ("w2T", [2048, 1024]), ("b2_8", [128, 8])]



def emit_l0_attn_b(pb, cst, W, xin, xout, dbg=None):
    C = 512
    NCH = T // C
    NH = 16
    sb = lambda n, s, d=F32: pb.sbuf("l0ab_" + n, s, d)

    Xc = [sb(f"x{k}", [128, C]) for k in range(8)]
    XNb = [sb(f"xnb{k}", [128, C], BF16) for k in range(8)]
    Qf = [sb(f"q{k}", [128, C], BF16) for k in range(8)]
    Kring = [sb(f"kr{k}", [128, 1024], BF16) for k in range(8)]
    Vring = [sb(f"vr{k}", [128, 1024], BF16) for k in range(8)]
    AOf = [sb(f"ao{k}", [128, C], BF16) for k in range(8)]
    Wv = [sb(f"wv{k}", [128, 1024], BF16) for k in range(8)]
    bvB = sb("bvB", [128, 1024])
    bv_row = sb("bv_row", [1, 1024])
    bq8 = sb("bq8", [128, 8]); bk8 = sb("bk8", [128, 8]); bo8 = sb("bo8", [128, 8])
    maskCat = sb("maskCat", [128, 384])
    mtmp = sb("mtmp", [128, 128])
    Sm = [sb(f"Sm{p}", [128, 384]) for p in range(2)]
    Se = [sb(f"Se{p}", [128, 384]) for p in range(2)]
    Sn = [sb(f"Sn{p}", [128, 384], BF16) for p in range(2)]
    STt = [sb(f"ST{p}", [128, 384], BF16) for p in range(2)]
    nmax = [sb(f"nmax{p}", [128, 1]) for p in range(2)]
    rsum = [sb(f"rsum{p}", [128, 1]) for p in range(2)]
    t4 = [sb(f"t4_{p}", [128, C]) for p in range(2)]
    outb = [sb(f"outb{p}", [128, C]) for p in range(2)]
    wbufs = [sb(f"wbuf{p}", [128, 1024], BF16) for p in range(2)]
    lnsc = {
        "tb": [sb(f"lntb{p}", [128, C], BF16) for p in range(2)],
        "tq": [sb(f"lntq{p}", [128, C], BF16) for p in range(2)],
        "rows": [sb(f"lnrow{i}", [1, C]) for i in range(4)],
        "ps_s1": pb.psum("l0ab_ps_s1", [128, C]),
        "ps_s2": pb.psum("l0ab_ps_s2", [128, C]),
        "psb0": pb.psum("l0ab_psb0", [128, C]),
        "psb1": pb.psum("l0ab_psb1", [128, C]),
    }
    pss = [pb.psum("l0ab_pp0", [128, C]), pb.psum("l0ab_pp1", [128, C])]
    ps_S = pb.psum("l0ab_psS", [128, 384])
    ps_T = pb.psum("l0ab_psT", [128, 384], BF16)

    def ldparams(e, sem):
        for name, dst in [("bq8", bq8), ("bk8", bk8), ("bo8", bo8)]:
            e.dma_start(out=dst[:], in_=W[name]).then_inc(sem, 16)
        e.dma_start(out=bv_row[:], in_=W["bv_row"]).then_inc(sem, 16)
    t_par = pb.dma("sync", ldparams, n=4, deps=[cst.ready])

    def ldwv(e, sem):
        for k in range(8):
            e.dma_start(out=Wv[k][:], in_=W["wv"][k * 128:(k + 1) * 128, :]).then_inc(sem, 16)
    t_wv = pb.dma("gpsimd", ldwv, n=8, deps=[cst.ready])

    def bvb_mm(t):
        t.matmul(out=pss[0][:, :512], lhsT=cst.onesf_row[:], rhs=bv_row[0:1, 0:512], start=True, stop=True)
        return t.matmul(out=pss[1][:, :512], lhsT=cst.onesf_row[:], rhs=bv_row[0:1, 512:1024], start=True, stop=True)
    t_bvm = pb.comp("tensor", bvb_mm, deps=[t_par])

    def bvb_cp(v):
        v.tensor_copy(out=bvB[:, 0:512], in_=pss[0][:, :512])
        return v.tensor_copy(out=bvB[:, 512:1024], in_=pss[1][:, :512])
    t_bvb = pb.comp("vector", bvb_cp, deps=[t_bvm])
    pb.state[("psum", id(pss[0]))] = [t_bvb, t_bvb]

    def mkmask(v):
        v.tensor_scalar(out=mtmp[:], in0=cst._iota_sc[:], scalar1=0.0, scalar2=None,
                        op0=OP.is_lt, op1=OP.bypass)
        v.tensor_scalar(out=maskCat[:, 0:128], in0=mtmp[:], scalar1=-1e30, scalar2=None,
                        op0=OP.mult, op1=OP.bypass)
        v.memset(maskCat[:, 128:256], 0.0)
        v.tensor_scalar(out=mtmp[:], in0=cst._iota_sc[:], scalar1=0.0, scalar2=None,
                        op0=OP.is_gt, op1=OP.bypass)
        return v.tensor_scalar(out=maskCat[:, 256:384], in0=mtmp[:], scalar1=-1e30, scalar2=None,
                               op0=OP.mult, op1=OP.bypass)
    t_mask = pb.comp("vector", mkmask, deps=[cst.ready])

    last_resadd = [None] * 8
    attn_tensor_last = [None, None]
    wo_mm_last = [None]
    dma_out_tok = [None, None]
    ch = {"stt": [None, None], "exp": [None, None], "norm": [None, None],
          "tr": [None, None], "stev": [None, None], "av": [None, None],
          "red": [None, None], "aoev": [None, None], "qk": [None, None]}

    for ci in range(NCH):
        t0 = ci * C
        half = ci % 2

        def ldx(e, sem, t0=t0):
            for k in range(8):
                e.dma_start(out=Xc[k][:], in_=xin[k * 128:(k + 1) * 128, t0:t0 + C]).then_inc(sem, 16)
        t_x = pb.dma("sync", ldx, n=8, deps=last_resadd)

        def ln_apply(k, psb0, psb1, dep):
            def vfn(v, k=k):
                v.tensor_tensor(out=t4[0][:], in0=Xc[k][:], in1=psb0[:, :C], op=OP.subtract)
                return v.tensor_tensor(out=XNb[k][:], in0=t4[0][:], in1=psb1[:, :C], op=OP.mult)
            return pb.comp("vector", vfn, deps=[dep])
        ln_toks, _ = emit_ln(pb, cst, lnsc, [x[:] for x in Xc], C, 8, ln_apply, xs_dep=t_x)

        def q_evac(i, ps, mmtok):
            return pb.comp("vector", lambda v, i=i: v.tensor_scalar(
                out=Qf[i][:], in0=ps[:, :C], scalar1=bq8[:, i:i + 1], scalar2=0.125,
                op0=OP.add, op1=OP.mult), deps=[mmtok, t_par, attn_tensor_last[(ci + 1) % 2]])
        q_toks = stream_matmul(pb, W["wq"], 8, 8, [x[:] for x in XNb], C, wbufs, pss,
                               q_evac, rhs_deps=ln_toks)

        def k_evac(i, ps, mmtok, half=half):
            return pb.comp("vector", lambda v, i=i, half=half: v.tensor_scalar(
                out=Kring[i][:, half * 512:(half + 1) * 512], in0=ps[:, :C],
                scalar1=bk8[:, i:i + 1], scalar2=None, op0=OP.add, op1=OP.bypass),
                deps=[mmtok, t_par, attn_tensor_last[(ci + 1) % 2]])
        k_toks = stream_matmul(pb, W["wk"], 8, 8, [x[:] for x in XNb], C, wbufs, pss,
                               k_evac, rhs_deps=ln_toks)

        st_p = pb.state[("psum", id(pss[0]))]
        v_toks = []
        for j in range(4):
            g = 4 * ci + j
            for hhalf in range(2):
                pw = (j * 2 + hhalf) % 2

                def vmm(t, j=j, hhalf=hhalf, pw=pw):
                    last = None
                    for dt in range(8):
                        last = t.matmul(out=pss[pw][:, :C], lhsT=XNb[dt][:, j * 128:(j + 1) * 128],
                                        rhs=Wv[dt][:, hhalf * 512:(hhalf + 1) * 512],
                                        start=(dt == 0), stop=(dt == 7))
                    return last
                t_vm = pb.comp("tensor", vmm, deps=[st_p[pw], t_wv] + ln_toks)

                def vev(v, g=g, hhalf=hhalf, pw=pw):
                    return v.tensor_tensor(out=Vring[g % 8][:, hhalf * 512:(hhalf + 1) * 512],
                                           in0=pss[pw][:, :C], in1=bvB[:, hhalf * 512:(hhalf + 1) * 512],
                                           op=OP.add)
                t_ve = pb.comp("vector", vev, deps=[t_vm, t_bvb, attn_tensor_last[(ci + 1) % 2]])
                st_p[pw] = t_ve
                v_toks.append(t_ve)

        it = 0
        for j in range(4):
            tq = 4 * ci + j
            nk = min(tq + 1, 3)
            k0t = tq - nk + 1
            for h in range(NH):
                p = it % 2
                e_t = h // 2
                po = (h % 2) * 64

                def qk(t, j=j, h=h, nk=nk, k0t=k0t, e_t=e_t, po=po):
                    last = None
                    for ii in range(nk):
                        gk = k0t + ii
                        col = (gk % 8) * 128
                        last = t.matmul(out=ps_S[:, ii * 128:(ii + 1) * 128],
                                        lhsT=Qf[e_t][po:po + 64, j * 128:(j + 1) * 128],
                                        rhs=Kring[e_t][po:po + 64, col:col + 128],
                                        start=True, stop=True)
                    return last
                t_qk = pb.comp("tensor", qk, deps=[q_toks[e_t], k_toks[e_t],
                                                   ch["stt"][0], ch["stt"][1]] + v_toks)

                def sttm(v, p=p, nk=nk):
                    return v.scalar_tensor_tensor(
                        out=Sm[p][:, :nk * 128], in0=ps_S[:, :nk * 128], scalar=1.0,
                        in1=maskCat[:, (3 - nk) * 128:384], op0=OP.mult, op1=OP.add)
                t_stt = pb.comp("vector", sttm, deps=[t_qk, t_mask, ch["exp"][0], ch["exp"][1]])
                ch["stt"][p] = t_stt

                def red(v, p=p, nk=nk):
                    return v.tensor_reduce(out=nmax[p][:], in_=Sm[p][:, :nk * 128],
                                           axis=AX.X, op=OP.max, negate=True)
                t_red = pb.comp("vector", red, deps=[t_stt])
                ch["red"][p] = t_red

                def expf(s, p=p, nk=nk):
                    return s.activation(out=Se[p][:, :nk * 128], in_=Sm[p][:, :nk * 128],
                                        func=AF.Exp, bias=nmax[p][:], accum_out=rsum[p][:])
                t_exp = pb.comp("scalar", expf, deps=[t_red, ch["norm"][p]])
                ch["exp"][p] = t_exp

                def norm(v, sem, base, p=p, nk=nk):
                    v.reciprocal(out=rsum[p][:], in_=rsum[p][:]).then_inc(sem, 1)
                    v.wait_ge(sem, base + 1)
                    v.tensor_scalar(out=Sn[p][:, :nk * 128], in0=Se[p][:, :nk * 128],
                                    scalar1=rsum[p][:], scalar2=None, op0=OP.mult,
                                    op1=OP.bypass).then_inc(sem, 1)
                t_norm = pb.step("vector", norm, 2, deps=[t_exp, ch["tr"][p]])
                ch["norm"][p] = t_norm

                def trs(t, p=p, nk=nk):
                    last = None
                    for ii in range(nk):
                        last = t.transpose(out=ps_T[:, ii * 128:(ii + 1) * 128],
                                           in_=Sn[p][:, ii * 128:(ii + 1) * 128],
                                           identity=cst.identb[:])
                    return last
                t_tr = pb.comp("tensor", trs, deps=[t_norm, ch["stev"][0], ch["stev"][1]])
                ch["tr"][p] = t_tr

                def stev(v, p=p, nk=nk):
                    return v.tensor_copy(out=STt[p][:, :nk * 128], in_=ps_T[:, :nk * 128])
                t_stev = pb.comp("vector", stev, deps=[t_tr, ch["av"][p]])
                ch["stev"][p] = t_stev

                pw = it % 2

                def av(t, p=p, nk=nk, k0t=k0t, h=h, pw=pw):
                    last = None
                    for ii in range(nk):
                        gk = k0t + ii
                        last = t.matmul(out=pss[pw][0:64, 0:128],
                                        lhsT=Vring[gk % 8][:, h * 64:(h + 1) * 64],
                                        rhs=STt[p][:, ii * 128:(ii + 1) * 128],
                                        start=(ii == 0), stop=(ii == nk - 1))
                    return last
                t_av = pb.comp("tensor", av, deps=[t_stev, st_p[pw]])
                ch["av"][p] = t_av
                attn_tensor_last[ci % 2] = t_av

                def aoev(v, h=h, j=j, pw=pw, e_t=e_t, po=po):
                    return v.tensor_copy(out=AOf[e_t][po:po + 64, j * 128:(j + 1) * 128],
                                         in_=pss[pw][0:64, 0:128])
                t_ao = pb.comp("vector", aoev, deps=[t_av, wo_mm_last[0]])
                st_p[pw] = t_ao
                ch["aoev"][p] = t_ao
                it += 1

        def wo_evac(i, ps, mmtok, ci=ci, t0=t0):
            p = i % 2

            def vfn(v, i=i, p=p):
                v.tensor_scalar(out=t4[p][:], in0=ps[:, :C], scalar1=bo8[:, i:i + 1],
                                scalar2=None, op0=OP.add, op1=OP.bypass)
                return v.tensor_tensor(out=outb[p][:], in0=t4[p][:], in1=Xc[i][:], op=OP.add)
            t_v = pb.comp("vector", vfn, deps=[mmtok, dma_out_tok[p], t_par])
            last_resadd[i] = t_v

            def st(e, sem, i=i, p=p, t0=t0):
                e.dma_start(out=xout[i * 128:(i + 1) * 128, t0:t0 + C], in_=outb[p][:]).then_inc(sem, 16)
            t_d = pb.dma("sync", st, n=1, deps=[t_v])
            dma_out_tok[p] = t_d
            return t_v
        wo_toks = stream_matmul(pb, W["wo"], 8, 8, [x[:] for x in AOf], C, wbufs, pss,
                                wo_evac, rhs_deps=[ch["aoev"][0], ch["aoev"][1]])
        wo_mm_last[0] = pb.state[("wbuf", id(wbufs[0]))][1] or pb.state[("wbuf", id(wbufs[0]))][0]

    if dbg is not None:
        named = {"q": Qf, "kr": Kring, "vr": Vring, "ao": AOf,
                 "sm": Sm, "se": Se, "sn": Sn, "st": STt}

        def dumps(e, sem):
            for key, ap in dbg.items():
                for i, tl in enumerate(named[key]):
                    e.dma_start(out=ap[i * 128:(i + 1) * 128, :], in_=tl[:]).then_inc(sem, 16)

        ndbg = sum(len(named[k]) for k in dbg)
        pb.dma("gpsimd", dumps, n=ndbg, deps=pb.all_toks())

    return pb.tok("sync")


def host_prep_l0_attn_b(inp):
    r = {}
    r["wq"] = inp["L0_attn_b_Wq"]
    r["wk"] = inp["L0_attn_b_Wk"]
    r["wv"] = inp["L0_attn_b_Wv"]
    r["wo"] = inp["L0_attn_b_Wo"]
    r["bq8"] = np.ascontiguousarray(inp["L0_attn_b_bq"].reshape(8, 128).T)
    r["bk8"] = np.ascontiguousarray(inp["L0_attn_b_bk"].reshape(8, 128).T)
    r["bo8"] = np.ascontiguousarray(inp["L0_attn_b_bo"].reshape(8, 128).T)
    r["bv_row"] = np.ascontiguousarray(inp["L0_attn_b_bv"].reshape(1, 1024))
    return r


L0AB_WSPECS = [("wq", [1024, 1024]), ("wk", [1024, 1024]), ("wv", [1024, 1024]),
               ("wo", [1024, 1024]), ("bq8", [128, 8]), ("bk8", [128, 8]),
               ("bo8", [128, 8]), ("bv_row", [1, 1024])]



def emit_l0_ffn(pb, cst, W, xin, xout, scr, tag="l0f_"):
    C = 512
    NCH = T // C
    NH = 8
    NE1 = 22
    sb = lambda n, s, d=F32: pb.sbuf(tag + n, s, d)

    Xi = [sb(f"xi{k}", [128, C]) for k in range(8)]
    hb = [sb(f"hb{k}", [128, C], BF16) for k in range(8)]
    Qr = [sb(f"qr{k}", [128, C]) for k in range(4)]
    Kr = [sb(f"kr{k}", [128, C]) for k in range(4)]
    Qf = [sb(f"qf{k}", [128, C], BF16) for k in range(4)]
    Kring = [sb(f"krng{k}", [128, 1024], BF16) for k in range(4)]
    Vring = [sb(f"vrng{k}", [128, C], BF16) for k in range(8)]
    AOf = [sb(f"ao{k}", [128, C], BF16) for k in range(4)]
    Gg = [sb(f"gg{k}", [128, C], BF16) for k in range(NE1)]
    Hh = [sb(f"hh{k}", [128, C], BF16) for k in range(NE1)]
    wqkv_v = [sb(f"wv{k}", [128, 512], BF16) for k in range(8)]
    csB = sb("csB", [128, C]); ssB = sb("ssB", [128, C])
    ciB = sb("ciB", [128, C]); siB = sb("siB", [128, C])
    Qsq = sb("qsq", [128, C], BF16)
    qn = sb("qn", [128, C]); rotb = sb("rotb", [128, C])
    tq1 = sb("tq1", [128, C]); tq2 = sb("tq2", [128, C])
    rn2 = sb("rn2", [128, C])
    bones = sb("bones", [128, 2], BF16)
    bonesf = sb("bonesf", [128, 2])
    onesf128 = sb("ones128", [128, 64])
    maskCat = sb("maskCat", [128, 640])
    mtmp = sb("mtmp", [128, 128])
    Sm = [sb(f"Sm{p}", [128, 640]) for p in range(2)]
    Se = [sb(f"Se{p}", [128, 640]) for p in range(2)]
    Sn = [sb(f"Sn{p}", [128, 640], BF16) for p in range(2)]
    STt = [sb(f"ST{p}", [128, 640], BF16) for p in range(2)]
    nmax = [sb(f"nmax{p}", [128, 1]) for p in range(2)]
    rsum = [sb(f"rsum{p}", [128, 1]) for p in range(2)]
    t4 = [sb(f"t4_{p}", [128, C]) for p in range(2)]
    outb = [sb(f"outb{p}", [128, C]) for p in range(2)]
    bcolA = [sb(f"bca{p}", [128, 4]) for p in range(2)]
    bcolF = [sb(f"bcf{p}", [128, 4]) for p in range(2)]
    wbufs = [sb(f"wbuf{p}", [128, 2816], BF16) for p in range(3)]
    params = {}
    for l in range(2):
        for nm, w_ in [("lg", 8), ("lb", 8), ("b1a", NE1), ("b1g", NE1), ("b2", 8),
                       ("qs", 1), ("ks", 1)]:
            params[f"{nm}_{l}"] = sb(f"p_{nm}{l}", [128, w_])
    lnps_a = pb.psum(tag + "lnA", [128, C])
    lnps_b = pb.psum(tag + "lnB", [128, C])
    lnsc = {
        "tb": [sb(f"lntb{p}", [128, C], BF16) for p in range(2)],
        "tq": [sb(f"lntq{p}", [128, C], BF16) for p in range(2)],
        "rows": [sb(f"lnrow{i}", [1, C]) for i in range(4)],
        "ps_s1": lnps_a, "ps_s2": lnps_b, "psb0": lnps_a, "psb1": lnps_b,
    }
    pss = [pb.psum(tag + "pp0", [128, C]), pb.psum(tag + "pp1", [128, C])]
    ps_S = pb.psum(tag + "psS", [128, 640])
    ps_T = pb.psum(tag + "psT", [128, 640], BF16)
    ps_n = pb.psum(tag + "psn", [128, C])

    def ldparams(e, sem):
        for l in range(2):
            for nm in ["lg", "lb", "b1a", "b1g", "b2", "qs", "ks"]:
                e.dma_start(out=params[f"{nm}_{l}"][:], in_=W[f"{nm}_{l}"]).then_inc(sem, 16)
    t_par = pb.dma("sync", ldparams, n=14, deps=[cst.ready])

    def mkmask(v):
        v.tensor_scalar(out=mtmp[:], in0=cst._iota_sc[:], scalar1=0.0, scalar2=None,
                        op0=OP.is_lt, op1=OP.bypass)
        v.tensor_scalar(out=maskCat[:, 0:128], in0=mtmp[:], scalar1=-1e30, scalar2=None,
                        op0=OP.mult, op1=OP.bypass)
        v.memset(maskCat[:, 128:512], 0.0)
        v.tensor_scalar(out=mtmp[:], in0=cst._iota_sc[:], scalar1=0.0, scalar2=None,
                        op0=OP.is_gt, op1=OP.bypass)
        return v.tensor_scalar(out=maskCat[:, 512:640], in0=mtmp[:], scalar1=-1e30, scalar2=None,
                               op0=OP.mult, op1=OP.bypass)
    t_mask = pb.comp("vector", mkmask, deps=[cst.ready])

    t_bi = pb.comp("gpsimd", lambda g: g.iota(out=mtmp[:, 0:2], pattern=[[0, 2]], base=0,
                                              channel_multiplier=1,
                                              allow_small_or_imprecise_dtypes=True),
                   deps=[t_mask])

    def mkbones(v):
        v.tensor_scalar(out=bonesf[:, 0:1], in0=mtmp[:, 0:1], scalar1=64.0, scalar2=None,
                        op0=OP.is_lt, op1=OP.bypass)
        v.tensor_scalar(out=bonesf[:, 1:2], in0=mtmp[:, 1:2], scalar1=63.0, scalar2=None,
                        op0=OP.is_gt, op1=OP.bypass)
        v.memset(onesf128[:], 1.0)
        return v.tensor_copy(out=bones[:], in_=bonesf[:])
    t_bones = pb.comp("vector", mkbones, deps=[t_bi])

    state = {
        "last_resadd": [None] * 8, "xt_reload": [None] * 8,
        "attn_t": [None, None], "wo_mm": [None],
        "dma_out": [None, None],
        "ch": {k: [None, None] for k in
               ["stt", "exp", "norm", "tr", "stev", "av", "aoev"]},
        "qkprep": [None],
        "hb_use": [None],
    }

    def inner_pass(l, src, dst, do_topln, add_xtop):
        ch = state["ch"]
        pqs = params[f"qs_{l}"]
        pks = params[f"ks_{l}"]

        def ldwv(e, sem):
            for k in range(8):
                e.dma_start(out=wqkv_v[k][:],
                            in_=W[f"wqkv_{l}"][k * 128:(k + 1) * 128, 1024:1536]).then_inc(sem, 16)
        t_wv = pb.dma("gpsimd", ldwv, n=8, deps=[pb.state.get(("wvlast",), None)])

        for ci in range(NCH):
            t0 = ci * C
            par = ci % 2

            def ldx(e, sem, t0=t0, src=src):
                for k in range(8):
                    e.dma_start(out=Xi[k][:], in_=src[k * 128:(k + 1) * 128, t0:t0 + C]).then_inc(sem, 16)
            t_x = pb.dma("sync", ldx, n=8, deps=state["last_resadd"] + [state["hb_use"][0]])

            def ldtab(e, sem, t0=t0):
                for nm, dstt in [("csT", csB), ("ssT", ssB), ("ciT", ciB), ("siT", siB)]:
                    e.dma_start(out=dstt[:], in_=W[nm][:, t0:t0 + C]).then_inc(sem, 16)
            t_tab = pb.dma("sync", ldtab, n=4, deps=[state["qkprep"][0]])

            if do_topln:
                def ln_apply(k, psb0, psb1, dep):
                    def vfn(v, k=k):
                        v.tensor_tensor(out=tq1[:], in0=Xi[k][:], in1=psb0[:, :C], op=OP.subtract)
                        return v.tensor_tensor(out=Xi[k][:], in0=tq1[:], in1=psb1[:, :C], op=OP.mult)
                    return pb.comp("vector", vfn, deps=[dep])
                ln_toks, _ = emit_ln(pb, cst, lnsc, [x[:] for x in Xi], C, 8, ln_apply, xs_dep=t_x)
                t_xr = ln_toks
            else:
                t_xr = [t_x]

            def snapA(v, par=par):
                last = None
                for k in range(4):
                    last = v.tensor_copy(out=bcolA[par][:, k:k + 1], in_=Xi[4 + k][:, C - 1:C])
                return last
            t_snapA = pb.comp("vector", snapA, deps=t_xr)

            def mkhb(v, ci=ci, par=par):
                for k in range(4, 8):
                    if ci == 0:
                        v.memset(hb[k][:, 0:1], 0.0)
                    else:
                        v.tensor_copy(out=hb[k][:, 0:1], in_=bcolA[1 - par][:, k - 4:k - 3])
                    v.tensor_copy(out=hb[k][:, 1:C], in_=Xi[k][:, 0:C - 1])
                last = None
                for k in range(4):
                    last = v.tensor_copy(out=hb[k][:], in_=Xi[k][:])
                return last
            t_hb = pb.comp("vector", mkhb, deps=t_xr + [t_snapA, state["hb_use"][0]])

            def qk_evac(i, ps, mmtok):
                dstt = Qr[i] if i < 4 else Kr[i - 4]
                return pb.comp("vector", lambda v, dstt=dstt: v.tensor_copy(out=dstt[:], in_=ps[:, :C]),
                               deps=[mmtok, state["qkprep"][0], state["xt_reload"][i]])
            qk_toks = stream_matmul(pb, W[f"wqkv_{l}"], 8, 8, [x[:] for x in hb], C,
                                    wbufs, pss, qk_evac, rhs_deps=[t_hb])

            st_p = pb.state[("psum", id(pss[0]))]
            v_toks = []
            for j in range(4):
                g = 4 * ci + j
                pw = j % 2

                def vmm(t, j=j, pw=pw):
                    last = None
                    for dt in range(8):
                        last = t.matmul(out=pss[pw][:, :C], lhsT=hb[dt][:, j * 128:(j + 1) * 128],
                                        rhs=wqkv_v[dt][:], start=(dt == 0), stop=(dt == 7))
                    return last
                t_vm = pb.comp("tensor", vmm, deps=[st_p[pw], t_wv, t_hb])

                def vev(v, g=g, pw=pw):
                    return v.tensor_copy(out=Vring[g % 8][:], in_=pss[pw][:, :C])
                t_ve = pb.comp("vector", vev, deps=[t_vm, state["attn_t"][(ci + 1) % 2]])
                st_p[pw] = t_ve
                v_toks.append(t_ve)

            prep_last = [state["qkprep"][0]]
            for which in range(2):
                for i in range(4):
                    srcap = (Qr if which == 0 else Kr)[i]

                    t_sq = pb.comp("scalar", lambda s, srcap=srcap: s.activation(
                        out=Qsq[:], in_=srcap[:], func=AF.Square),
                        deps=[qk_toks[which * 4 + i], prep_last[0]])

                    def ssmm(t):
                        t.matmul(out=ps_n[0:1, :C], lhsT=bones[:, 0:1], rhs=Qsq[:],
                                 start=True, stop=True)
                        return t.matmul(out=ps_n[64:65, :C], lhsT=bones[:, 1:2], rhs=Qsq[:],
                                        start=True, stop=True)
                    t_ss = pb.comp("tensor", ssmm, deps=[t_sq, t_bones, prep_last[0]])

                    t_eps = pb.comp("vector", lambda v: v.tensor_scalar(
                        out=rn2[:], in0=ps_n[:, :C], scalar1=1e-24, scalar2=None,
                        op0=OP.add, op1=OP.bypass), deps=[t_ss])

                    def rnf(s):
                        s.activation(out=rn2[:], in_=rn2[:], func=AF.Ln)
                        return s.activation(out=rn2[:], in_=rn2[:], func=AF.Exp, scale=-0.5)
                    t_rn = pb.comp("scalar", rnf, deps=[t_eps])

                    def rbc(t):
                        t.matmul(out=ps_n[0:64, :C], lhsT=cst.onesf_row[:, 0:64], rhs=rn2[0:1, :C],
                                 start=True, stop=True)
                        return t.matmul(out=ps_n[64:128, :C], lhsT=onesf128[64:65, :],
                                        rhs=rn2[64:65, :C], start=True, stop=True)
                    t_rb = pb.comp("tensor", rbc, deps=[t_rn])

                    def prep(v, sem, base, which=which, i=i, ci=ci):
                        srcap = (Qr if which == 0 else Kr)[i]
                        sc1 = pqs if which == 0 else pks
                        sc2 = 8.0 if which == 0 else None
                        v.tensor_tensor(out=qn[:], in0=srcap[:], in1=ps_n[:, :C], op=OP.mult)
                        if which == 0:
                            v.tensor_scalar(out=qn[:], in0=qn[:], scalar1=sc1[:], scalar2=8.0,
                                            op0=OP.mult, op1=OP.mult)
                        else:
                            v.tensor_scalar(out=qn[:], in0=qn[:], scalar1=sc1[:], scalar2=None,
                                            op0=OP.mult, op1=OP.bypass)
                        v.tensor_scalar(out=rotb[0:32, :], in0=qn[32:64, :], scalar1=-1.0,
                                        scalar2=None, op0=OP.mult, op1=OP.bypass)
                        v.tensor_copy(out=rotb[32:64, :], in_=qn[0:32, :])
                        v.tensor_scalar(out=rotb[64:96, :], in0=qn[96:128, :], scalar1=-1.0,
                                        scalar2=None, op0=OP.mult, op1=OP.bypass)
                        v.tensor_copy(out=rotb[96:128, :], in_=qn[64:96, :])
                        cB = csB if which == 0 else ciB
                        sB = ssB if which == 0 else siB
                        v.tensor_tensor(out=tq1[:], in0=qn[:], in1=cB[:], op=OP.mult)
                        v.tensor_tensor(out=tq2[:], in0=rotb[:], in1=sB[:], op=OP.mult)
                        if which == 0:
                            v.tensor_tensor(out=Qf[i][:], in0=tq1[:], in1=tq2[:],
                                            op=OP.add).then_inc(sem, 1)
                        else:
                            col = ((4 * ci) % 8) * 128
                            v.tensor_tensor(out=Kring[i][:, col:col + C], in0=tq1[:],
                                            in1=tq2[:], op=OP.add).then_inc(sem, 1)
                    t_pp = pb.step("vector", prep, 1,
                                   deps=[t_rb, t_tab, t_par, state["attn_t"][(ci + 1) % 2]])
                    prep_last[0] = t_pp
            state["qkprep"][0] = prep_last[0]

            it = 0
            for j in range(4):
                tq = 4 * ci + j
                nk = min(tq + 1, 5)
                k0t = tq - nk + 1
                for h in range(NH):
                    p = it % 2
                    e_t = h // 2
                    po = (h % 2) * 64

                    def qk(t, j=j, nk=nk, k0t=k0t, e_t=e_t, po=po):
                        last = None
                        for ii in range(nk):
                            gk = k0t + ii
                            col = (gk % 8) * 128
                            last = t.matmul(out=ps_S[:, ii * 128:(ii + 1) * 128],
                                            lhsT=Qf[e_t][po:po + 64, j * 128:(j + 1) * 128],
                                            rhs=Kring[e_t][po:po + 64, col:col + 128],
                                            start=True, stop=True)
                        return last
                    t_qk = pb.comp("tensor", qk, deps=[prep_last[0], ch["stt"][0], ch["stt"][1]])

                    def sttm(v, p=p, nk=nk):
                        return v.scalar_tensor_tensor(
                            out=Sm[p][:, :nk * 128], in0=ps_S[:, :nk * 128], scalar=1.0,
                            in1=maskCat[:, (5 - nk) * 128:640], op0=OP.mult, op1=OP.add)
                    t_stt = pb.comp("vector", sttm, deps=[t_qk, t_mask, ch["exp"][0], ch["exp"][1]])
                    ch["stt"][p] = t_stt

                    t_red = pb.comp("vector", lambda v, p=p, nk=nk: v.tensor_reduce(
                        out=nmax[p][:], in_=Sm[p][:, :nk * 128], axis=AX.X, op=OP.max, negate=True),
                        deps=[t_stt])

                    t_exp = pb.comp("scalar", lambda s, p=p, nk=nk: s.activation(
                        out=Se[p][:, :nk * 128], in_=Sm[p][:, :nk * 128], func=AF.Exp,
                        bias=nmax[p][:], accum_out=rsum[p][:]),
                        deps=[t_red, ch["norm"][p]])
                    ch["exp"][p] = t_exp

                    def norm(v, sem, base, p=p, nk=nk):
                        v.reciprocal(out=rsum[p][:], in_=rsum[p][:]).then_inc(sem, 1)
                        v.wait_ge(sem, base + 1)
                        v.tensor_scalar(out=Sn[p][:, :nk * 128], in0=Se[p][:, :nk * 128],
                                        scalar1=rsum[p][:], scalar2=None, op0=OP.mult,
                                        op1=OP.bypass).then_inc(sem, 1)
                    t_norm = pb.step("vector", norm, 2, deps=[t_exp, ch["tr"][p]])
                    ch["norm"][p] = t_norm

                    def trs(t, p=p, nk=nk):
                        last = None
                        for ii in range(nk):
                            last = t.transpose(out=ps_T[:, ii * 128:(ii + 1) * 128],
                                               in_=Sn[p][:, ii * 128:(ii + 1) * 128],
                                               identity=cst.identb[:])
                        return last
                    t_tr = pb.comp("tensor", trs, deps=[t_norm, ch["stev"][0], ch["stev"][1]])
                    ch["tr"][p] = t_tr

                    t_stev = pb.comp("vector", lambda v, p=p, nk=nk: v.tensor_copy(
                        out=STt[p][:, :nk * 128], in_=ps_T[:, :nk * 128]),
                        deps=[t_tr, ch["av"][p]])
                    ch["stev"][p] = t_stev

                    pw = it % 2

                    def av(t, p=p, nk=nk, k0t=k0t, h=h, pw=pw):
                        last = None
                        for ii in range(nk):
                            gk = k0t + ii
                            last = t.matmul(out=pss[pw][0:64, 0:128],
                                            lhsT=Vring[gk % 8][:, h * 64:(h + 1) * 64],
                                            rhs=STt[p][:, ii * 128:(ii + 1) * 128],
                                            start=(ii == 0), stop=(ii == nk - 1))
                        return last
                    t_av = pb.comp("tensor", av, deps=[t_stev, st_p[pw]])
                    ch["av"][p] = t_av
                    state["attn_t"][ci % 2] = t_av

                    t_ao = pb.comp("vector", lambda v, h=h, j=j, pw=pw, e_t=e_t, po=po: v.tensor_copy(
                        out=AOf[e_t][po:po + 64, j * 128:(j + 1) * 128], in_=pss[pw][0:64, 0:128]),
                        deps=[t_av, state["wo_mm"][0]])
                    st_p[pw] = t_ao
                    ch["aoev"][p] = t_ao
                    it += 1

            def wo_evac(i, ps, mmtok):
                def vfn(v, i=i):
                    return v.tensor_tensor(out=Xi[i][:], in0=ps[:, :C], in1=Xi[i][:], op=OP.add)
                return pb.comp("vector", vfn, deps=[mmtok])
            wo_toks = stream_matmul(pb, W[f"wo_{l}"], 4, 8, [x[:] for x in AOf], C,
                                    wbufs, pss, wo_evac,
                                    rhs_deps=[ch["aoev"][0], ch["aoev"][1]])
            stw = pb.state[("wbuf", id(wbufs[0]))]
            state["wo_mm"][0] = stw[1]

            def snapF(v, par=par):
                last = None
                for k in range(4):
                    last = v.tensor_copy(out=bcolF[par][:, k:k + 1], in_=Xi[4 + k][:, C - 1:C])
                return last
            t_snapF = pb.comp("vector", snapF, deps=wo_toks)

            def mkhf(v, ci=ci, par=par):
                for k in range(4, 8):
                    if ci == 0:
                        v.memset(hb[k][:, 0:1], 0.0)
                    else:
                        v.tensor_copy(out=hb[k][:, 0:1], in_=bcolF[1 - par][:, k - 4:k - 3])
                    v.tensor_copy(out=hb[k][:, 1:C], in_=Xi[k][:, 0:C - 1])
                last = None
                for k in range(4):
                    last = v.tensor_copy(out=hb[k][:], in_=Xi[k][:])
                return last
            t_hf = pb.comp("vector", mkhf, deps=wo_toks + [t_snapF, state["hb_use"][0]])

            def lnf_apply(k, psb0, psb1, dep):
                def vfn(v, k=k, l=l):
                    v.tensor_tensor(out=tq1[:], in0=hb[k][:], in1=psb0[:, :C], op=OP.subtract)
                    v.tensor_tensor(out=tq2[:], in0=tq1[:], in1=psb1[:, :C], op=OP.mult)
                    return v.tensor_scalar(out=hb[k][:], in0=tq2[:],
                                           scalar1=params[f"lg_{l}"][:, k:k + 1],
                                           scalar2=params[f"lb_{l}"][:, k:k + 1],
                                           op0=OP.mult, op1=OP.add)
                return pb.comp("vector", vfn, deps=[dep, t_par])
            lnf_toks, _ = emit_ln(pb, cst, lnsc, [x[:] for x in hb], C, 8, lnf_apply,
                                  xs_dep=t_hf, xs_are_bf16=True)

            def g_evac(i, ps, mmtok):
                return pb.comp("scalar", lambda s, i=i, l=l: s.activation(
                    out=Gg[i][:], in_=ps[:, :C], func=AF.Gelu,
                    bias=params[f"b1g_{l}"][:, i:i + 1]), deps=[mmtok, t_par])
            g_toks = stream_matmul(pb, W[f"w1g_{l}"], 8, NE1, [x[:] for x in hb], C,
                                   wbufs, pss, g_evac, rhs_deps=lnf_toks)

            def a_evac(i, ps, mmtok):
                def vfn(v, i=i, l=l):
                    v.tensor_scalar(out=tq1[:], in0=ps[:, :C],
                                    scalar1=params[f"b1a_{l}"][:, i:i + 1], scalar2=None,
                                    op0=OP.add, op1=OP.bypass)
                    return v.tensor_tensor(out=Hh[i][:], in0=tq1[:], in1=Gg[i][:], op=OP.mult)
                return pb.comp("vector", vfn, deps=[mmtok, g_toks[i], t_par])
            a_toks = stream_matmul(pb, W[f"w1a_{l}"], 8, NE1, [x[:] for x in hb], C,
                                   wbufs, pss, a_evac, rhs_deps=lnf_toks)
            stw2 = pb.state[("wbuf", id(wbufs[0]))]
            state["hb_use"][0] = stw2[1]

            def w2_evac(i, ps, mmtok, t0=t0, dst=dst, add_xtop=add_xtop):
                p = i % 2
                if add_xtop:
                    def ldxt(e, sem, i=i, t0=t0):
                        e.dma_start(out=(Qr + Kr)[i][:],
                                    in_=xin[i * 128:(i + 1) * 128, t0:t0 + C]).then_inc(sem, 16)
                    t_xt = pb.dma("sync", ldxt, n=1, deps=[state["qkprep"][0]])
                else:
                    t_xt = None

                def vfn(v, i=i, p=p, l=l, add_xtop=add_xtop):
                    v.tensor_scalar(out=t4[p][:], in0=ps[:, :C],
                                    scalar1=params[f"b2_{l}"][:, i:i + 1], scalar2=None,
                                    op0=OP.add, op1=OP.bypass)
                    v.tensor_tensor(out=t4[p][:], in0=t4[p][:], in1=Xi[i][:], op=OP.add)
                    if add_xtop:
                        return v.tensor_tensor(out=outb[p][:], in0=t4[p][:], in1=(Qr + Kr)[i][:],
                                               op=OP.add)
                    else:
                        return v.tensor_copy(out=outb[p][:], in_=t4[p][:])
                t_v = pb.comp("vector", vfn, deps=[mmtok, state["dma_out"][p], t_xt, t_par])
                state["last_resadd"][i] = t_v
                if add_xtop:
                    state["xt_reload"][i] = t_v

                def st(e, sem, i=i, p=p, t0=t0, dst=dst):
                    e.dma_start(out=dst[i * 128:(i + 1) * 128, t0:t0 + C], in_=outb[p][:]).then_inc(sem, 16)
                t_d = pb.dma("sync", st, n=1, deps=[t_v])
                state["dma_out"][p] = t_d
                return t_v
            w2_toks = stream_matmul(pb, W[f"w2p_{l}"], NE1, 8, [x[:] for x in Hh], C,
                                    wbufs, pss, w2_evac, rhs_deps=a_toks)
        pb.state[("wvlast",)] = pb.tok("tensor")

    inner_pass(0, xin, scr["xa"], True, False)
    inner_pass(1, scr["xa"], xout, False, True)
    return pb.tok("sync")


def _l0ffn_rotary_tables():
    DIM_HEAD, WINDOW, n = 64, 512, 4096
    half = np.arange(0, DIM_HEAD, 2, dtype=np.float64)
    inv_freq = 1.0 / 10000.0 ** (half / DIM_HEAD)
    t = np.arange(n, dtype=np.float64)
    freqs = t[:, None] * inv_freq[None, :]
    freqs = np.concatenate([freqs, freqs], axis=-1)
    base = (half + 0.4 * DIM_HEAD) / (1.4 * DIM_HEAD)
    power = (t - n // 2) / float(WINDOW // 2)
    scale = base[None, :] ** power[:, None]
    scale = np.concatenate([scale, scale], axis=-1)
    cs = (np.cos(freqs) * scale).astype(np.float32)
    ss = (np.sin(freqs) * scale).astype(np.float32)
    ci = (np.cos(freqs) / scale).astype(np.float32)
    si = (np.sin(freqs) / scale).astype(np.float32)
    fm = lambda a: np.ascontiguousarray(np.vstack([a.T, a.T]))
    return {"csT": fm(cs), "ssT": fm(ss), "ciT": fm(ci), "siT": fm(si)}


def host_prep_l0_ffn(inp):
    r = _l0ffn_rotary_tables()
    HID = 2730
    for l in range(2):
        wqkv = inp["L0_ffn_wqkv"][l]
        r[f"wqkv_{l}"] = np.ascontiguousarray(wqkv)
        r[f"wo_{l}"] = np.ascontiguousarray(inp["L0_ffn_wo"][l])
        w1 = inp["L0_ffn_w1"][l]
        w1a = np.zeros((1024, 2816), np.float32); w1a[:, :HID] = w1[:, :HID]
        w1g = np.zeros((1024, 2816), np.float32); w1g[:, :HID] = w1[:, HID:]
        r[f"w1a_{l}"] = w1a
        r[f"w1g_{l}"] = w1g
        w2p = np.zeros((2816, 1024), np.float32); w2p[:HID] = inp["L0_ffn_w2"][l]
        r[f"w2p_{l}"] = w2p
        b1 = inp["L0_ffn_b1"][l]
        b1a = np.zeros(2816, np.float32); b1a[:HID] = b1[:HID]
        b1g = np.zeros(2816, np.float32); b1g[:HID] = b1[HID:]
        r[f"b1a_{l}"] = np.ascontiguousarray(b1a.reshape(22, 128).T)
        r[f"b1g_{l}"] = np.ascontiguousarray(b1g.reshape(22, 128).T)
        r[f"b2_{l}"] = np.ascontiguousarray(inp["L0_ffn_b2"][l].reshape(8, 128).T)
        r[f"lg_{l}"] = np.ascontiguousarray(inp["L0_ffn_ln_g"][l].reshape(8, 128).T)
        r[f"lb_{l}"] = np.ascontiguousarray(inp["L0_ffn_ln_b"][l].reshape(8, 128).T)
        r[f"qs_{l}"] = np.ascontiguousarray(np.tile(inp["L0_ffn_q_scale"][l], 2).reshape(128, 1))
        r[f"ks_{l}"] = np.ascontiguousarray(np.tile(inp["L0_ffn_k_scale"][l], 2).reshape(128, 1))
    return r


L0F_WSPECS = ([("csT", [128, 4096]), ("ssT", [128, 4096]), ("ciT", [128, 4096]),
               ("siT", [128, 4096])] +
              [(f"{nm}_{l}", shp) for l in range(2) for nm, shp in
               [("wqkv", [1024, 1536]), ("wo", [512, 1024]), ("w1a", [1024, 2816]),
                ("w1g", [1024, 2816]), ("w2p", [2816, 1024]), ("b1a", [128, 22]),
                ("b1g", [128, 22]), ("b2", [128, 8]), ("lg", [128, 8]),
                ("lb", [128, 8]), ("qs", [128, 1]), ("ks", [128, 1])]])



def emit_l1_attn_b(pb, cst, W, xin, xout):
    C = 512
    NCH = T // C
    sb = lambda n, s, d=F32: pb.sbuf("l1b_" + n, s, d)

    Xc = [sb(f"x{k}", [128, C]) for k in range(8)]
    XN = [sb(f"xn{k}", [128, C]) for k in range(8)]
    Yb = [sb(f"yb{k}", [128, C], BF16) for k in range(8)]
    Qs = [sb(f"qs{k}", [128, C]) for k in range(8)]
    Kf = [sb(f"kf{k}", [128, C]) for k in range(8)]
    Gs = [sb(f"gs{k}", [128, C]) for k in range(8)]
    Of = [sb(f"of{k}", [128, C], BF16) for k in range(8)]
    X2 = [sb(f"x2_{k}", [128, C]) for k in range(8)]
    H2b = [sb(f"h2_{k}", [128, C], BF16) for k in range(8)]
    F1 = [sb(f"f1_{k}", [128, C], BF16) for k in range(32)]
    kvT = sb("kv", [128, C]); sT = sb("sT", [128, C]); o1T = sb("o1", [128, C])
    decT = sb("decT", [128, C])
    dsig = sb("dsig", [128, 8])
    carry = sb("carry", [128, 8])
    t4 = [sb(f"t4_{p}", [128, C]) for p in range(2)]
    outb = [sb(f"outb{p}", [128, C]) for p in range(2)]
    wbufs = [sb(f"wbuf{p}", [128, 4096], BF16) for p in range(2)]
    P = {}
    for nm, w_ in [("bq8", 8), ("bk8", 8), ("bv8", 8), ("bg8", 8), ("bo8", 8),
                   ("b2_8", 8), ("b1_32", 32), ("ln1g8", 8), ("ln1b8", 8),
                   ("ln2g8", 8), ("ln2b8", 8), ("dl8", 8)]:
        P[nm] = sb("p_" + nm, [128, w_])
    lnps_a = pb.psum("l1b_lnA", [128, C])
    lnps_b = pb.psum("l1b_lnB", [128, C])
    lnsc = {
        "tb": [sb(f"lntb{p}", [128, C], BF16) for p in range(2)],
        "tq": [sb(f"lntq{p}", [128, C], BF16) for p in range(2)],
        "rows": [sb(f"lnrow{i}", [1, C]) for i in range(4)],
        "ps_s1": lnps_a, "ps_s2": lnps_b, "psb0": lnps_a, "psb1": lnps_b,
    }
    pss = [pb.psum("l1b_pp0", [128, C]), pb.psum("l1b_pp1", [128, C])]

    def ldparams(e, sem):
        for nm in P:
            e.dma_start(out=P[nm][:], in_=W[nm]).then_inc(sem, 16)
    t_par = pb.dma("sync", ldparams, n=len(P), deps=[cst.ready])
    t_dsig = pb.comp("scalar", lambda s: s.activation(out=dsig[:], in_=P["dl8"][:],
                                                      func=AF.Sigmoid), deps=[t_par])

    state = {"last_out": [None] * 8, "dma_out": [None, None], "scan": [None],
             "proj_use": [None], "qkg_use": [None]}

    for ci in range(NCH):
        t0 = ci * C

        def ldx(e, sem, t0=t0):
            for k in range(8):
                e.dma_start(out=Xc[k][:], in_=xin[k * 128:(k + 1) * 128, t0:t0 + C]).then_inc(sem, 16)
        t_x = pb.dma("sync", ldx, n=8, deps=state["last_out"])

        def ln_top_apply(k, psb0, psb1, dep):
            def vfn(v, k=k):
                v.tensor_tensor(out=t4[0][:], in0=Xc[k][:], in1=psb0[:, :C], op=OP.subtract)
                return v.tensor_tensor(out=XN[k][:], in0=t4[0][:], in1=psb1[:, :C], op=OP.mult)
            return pb.comp("vector", vfn, deps=[dep, state["last_out"][k]])
        lnt_toks, _ = emit_ln(pb, cst, lnsc, [x[:] for x in Xc], C, 8, ln_top_apply, xs_dep=t_x)

        def ln1_apply(k, psb0, psb1, dep):
            def vfn(v, k=k):
                v.tensor_tensor(out=t4[0][:], in0=XN[k][:], in1=psb0[:, :C], op=OP.subtract)
                v.tensor_tensor(out=t4[1][:], in0=t4[0][:], in1=psb1[:, :C], op=OP.mult)
                return v.tensor_scalar(out=Yb[k][:], in0=t4[1][:],
                                       scalar1=P["ln1g8"][:, k:k + 1],
                                       scalar2=P["ln1b8"][:, k:k + 1], op0=OP.mult, op1=OP.add)
            return pb.comp("vector", vfn, deps=[dep, t_par, state["proj_use"][0]])
        ln1_toks, _ = emit_ln(pb, cst, lnsc, [x[:] for x in XN], C, 8, ln1_apply,
                              xs_dep=lnt_toks[-1])

        def q_evac(i, ps, mmtok):
            return pb.comp("vector", lambda v, i=i: v.tensor_scalar(
                out=Qs[i][:], in0=ps[:, :C], scalar1=P["bq8"][:, i:i + 1], scalar2=0.125,
                op0=OP.add, op1=OP.mult), deps=[mmtok, t_par, state["qkg_use"][0]])
        q_toks = stream_matmul(pb, W["wq"], 8, 8, [x[:] for x in Yb], C, wbufs, pss,
                               q_evac, rhs_deps=ln1_toks)

        def k_evac(i, ps, mmtok):
            return pb.comp("vector", lambda v, i=i: v.tensor_scalar(
                out=Kf[i][:], in0=ps[:, :C], scalar1=P["bk8"][:, i:i + 1], scalar2=None,
                op0=OP.add, op1=OP.bypass), deps=[mmtok, t_par, state["qkg_use"][0]])
        k_toks = stream_matmul(pb, W["wk"], 8, 8, [x[:] for x in Yb], C, wbufs, pss,
                               k_evac, rhs_deps=ln1_toks)

        def g_evac(i, ps, mmtok):
            return pb.comp("scalar", lambda s, i=i: s.activation(
                out=Gs[i][:], in_=ps[:, :C], func=AF.Sigmoid, bias=P["bg8"][:, i:i + 1]),
                deps=[mmtok, t_par, state["qkg_use"][0]])
        g_toks = stream_matmul(pb, W["wg"], 8, 8, [x[:] for x in Yb], C, wbufs, pss,
                               g_evac, rhs_deps=ln1_toks)

        def v_evac(i, ps, mmtok, ci=ci):
            def vfn(v, sem, base, i=i, ci=ci):
                v.tensor_scalar(out=kvT[:], in0=ps[:, :C], scalar1=P["bv8"][:, i:i + 1],
                                scalar2=None, op0=OP.add, op1=OP.bypass)
                v.tensor_tensor(out=kvT[:], in0=kvT[:], in1=Kf[i][:], op=OP.mult)
                v.tensor_scalar(out=decT[:], in0=cst.ones_tile[:, :C],
                                scalar1=dsig[:, i:i + 1], scalar2=None, op0=OP.mult, op1=OP.bypass)
                init = 0.0 if ci == 0 else carry[:, i:i + 1]
                v.tensor_tensor_scan(out=sT[:], data0=decT[:], data1=kvT[:], initial=init,
                                     op0=OP.mult, op1=OP.add).then_inc(sem, 1)
                v.wait_ge(sem, base + 1)
                v.tensor_copy(out=carry[:, i:i + 1], in_=sT[:, C - 1:C])
                v.tensor_tensor(out=o1T[:], in0=sT[:], in1=Qs[i][:], op=OP.mult)
                v.tensor_tensor(out=Of[i][:], in0=o1T[:], in1=Gs[i][:], op=OP.mult).then_inc(sem, 1)
            return pb.step("vector", vfn, 2, deps=[mmtok, t_dsig, q_toks[i], k_toks[i],
                                                   g_toks[i], state["scan"][0]])
        v_toks = stream_matmul(pb, W["wv"], 8, 8, [x[:] for x in Yb], C, wbufs, pss,
                               v_evac, rhs_deps=ln1_toks)
        state["scan"][0] = v_toks[-1]
        state["qkg_use"][0] = v_toks[-1]

        def wo_evac(i, ps, mmtok):
            def vfn(v, i=i):
                v.tensor_scalar(out=t4[0][:], in0=ps[:, :C], scalar1=P["bo8"][:, i:i + 1],
                                scalar2=None, op0=OP.add, op1=OP.bypass)
                return v.tensor_tensor(out=X2[i][:], in0=t4[0][:], in1=XN[i][:], op=OP.add)
            return pb.comp("vector", vfn, deps=[mmtok, t_par])
        wo_toks = stream_matmul(pb, W["wo"], 8, 8, [x[:] for x in Of], C, wbufs, pss,
                                wo_evac, rhs_deps=v_toks)
        state["proj_use"][0] = pb.state[("wbuf", id(wbufs[0]))][1]

        def ln2_apply(k, psb0, psb1, dep):
            def vfn(v, k=k):
                v.tensor_tensor(out=t4[0][:], in0=X2[k][:], in1=psb0[:, :C], op=OP.subtract)
                v.tensor_tensor(out=t4[1][:], in0=t4[0][:], in1=psb1[:, :C], op=OP.mult)
                return v.tensor_scalar(out=H2b[k][:], in0=t4[1][:],
                                       scalar1=P["ln2g8"][:, k:k + 1],
                                       scalar2=P["ln2b8"][:, k:k + 1], op0=OP.mult, op1=OP.add)
            return pb.comp("vector", vfn, deps=[dep, t_par])
        ln2_toks, _ = emit_ln(pb, cst, lnsc, [x[:] for x in X2], C, 8, ln2_apply,
                              xs_dep=wo_toks[-1])

        def f1_evac(i, ps, mmtok):
            return pb.comp("scalar", lambda s, i=i: s.activation(
                out=F1[i][:], in_=ps[:, :C], func=AF.Gelu, bias=P["b1_32"][:, i:i + 1]),
                deps=[mmtok, t_par])
        f1_toks = stream_matmul(pb, W["w1"], 8, 32, [x[:] for x in H2b], C, wbufs, pss,
                                f1_evac, rhs_deps=ln2_toks)

        def w2_evac(i, ps, mmtok, t0=t0):
            p = i % 2

            def vfn(v, i=i, p=p):
                v.tensor_scalar(out=t4[p][:], in0=ps[:, :C], scalar1=P["b2_8"][:, i:i + 1],
                                scalar2=None, op0=OP.add, op1=OP.bypass)
                v.tensor_tensor(out=t4[p][:], in0=t4[p][:], in1=X2[i][:], op=OP.add)
                return v.tensor_tensor(out=outb[p][:], in0=t4[p][:], in1=Xc[i][:], op=OP.add)
            t_v = pb.comp("vector", vfn, deps=[mmtok, state["dma_out"][p], t_par])
            state["last_out"][i] = t_v

            def st(e, sem, i=i, p=p, t0=t0):
                e.dma_start(out=xout[i * 128:(i + 1) * 128, t0:t0 + C], in_=outb[p][:]).then_inc(sem, 16)
            t_d = pb.dma("sync", st, n=1, deps=[t_v])
            state["dma_out"][p] = t_d
            return t_v
        w2_toks = stream_matmul(pb, W["w2"], 32, 8, [x[:] for x in F1], C, wbufs, pss,
                                w2_evac, rhs_deps=f1_toks)

    return pb.tok("sync")


def host_prep_l1_attn_b(inp):
    r = {}
    pre = "L1_attn_b_"
    for src, dst in [("Wq", "wq"), ("Wk", "wk"), ("Wv", "wv"), ("Wg", "wg"),
                     ("Wo", "wo"), ("W1", "w1"), ("W2", "w2")]:
        r[dst] = inp[pre + src]
    col8 = lambda a: np.ascontiguousarray(a.reshape(-1, 128).T)
    for src, dst in [("bq", "bq8"), ("bk", "bk8"), ("bv", "bv8"), ("bg", "bg8"),
                     ("bo", "bo8"), ("b2", "b2_8"), ("b1", "b1_32"),
                     ("ln1_g", "ln1g8"), ("ln1_b", "ln1b8"),
                     ("ln2_g", "ln2g8"), ("ln2_b", "ln2b8")]:
        r[dst] = col8(inp[pre + src])
    r["dl8"] = col8(np.repeat(inp[pre + "decay_logit"], 64))
    return r


L1B_WSPECS = ([("wq", [1024, 1024]), ("wk", [1024, 1024]), ("wv", [1024, 1024]),
               ("wg", [1024, 1024]), ("wo", [1024, 1024]), ("w1", [1024, 4096]),
               ("w2", [4096, 1024])] +
              [(n, [128, 8]) for n in ["bq8", "bk8", "bv8", "bg8", "bo8", "b2_8",
                                       "ln1g8", "ln1b8", "ln2g8", "ln2b8", "dl8"]] +
              [("b1_32", [128, 32])])



def emit_l1_ffn(pb, cst, W, xin, xout, scr, tag):
    nc = pb.nc
    C = 512
    NCH = T // C
    RG = [list(range(8))]

    with ExitStack() as es1:
        old_es = pb.es
        pb.es = es1
        sb = lambda n, s, d=F32: pb.sbuf(tag + "p1_" + n, s, d)
        Xc = [sb(f"x{k}", [128, C]) for k in range(8)]
        XNb = [sb(f"xnb{k}", [128, C], BF16) for k in range(8)]
        phiT = [sb(f"phiT{k}", [128, 2048], BF16) for k in range(8)]
        bphiB = sb("bphiB", [128, 2048])
        phib_row = sb("phib_row", [1, 2048])
        Lg = sb("Lg", [128, 2048])
        Eg = sb("Eg", [128, 2048])
        Db = sb("Db", [128, 2048], BF16)
        Cb = sb("Cb", [128, 2048], BF16)
        XTst = sb("XTst", [128, 1024], BF16)
        CTs = [sb(f"CTs{i}", [128, C], BF16) for i in range(16)]
        nmax = sb("nmax", [128, 1])
        sumD = sb("sumD", [128, 64])
        sumC = sb("sumC", [128, 32])
        lnsc = {
            "tb": [sb(f"lntb{p}", [128, C], BF16) for p in range(2)],
            "tq": [sb(f"lntq{p}", [128, C], BF16) for p in range(2)],
            "rows": [sb(f"lnrow{i}", [1, C]) for i in range(4)],
        }
        lnsc["ps_s1"] = lnsc["psb0"] = pb.psum(tag + "lnA", [128, C])
        lnsc["ps_s2"] = lnsc["psb1"] = pb.psum(tag + "lnB", [128, C])
        pss = [pb.psum(tag + "pp0", [128, C]), pb.psum(tag + "pp1", [128, C])]
        ps_T = pb.psum(tag + "psT", [128, 2048], BF16)
        tA = sb("tA", [128, C])

        def ldphi(e, sem):
            for k in range(8):
                e.dma_start(out=phiT[k][:], in_=W["phiT"][k * 128:(k + 1) * 128, :]).then_inc(sem, 16)
        t_phi = pb.dma("gpsimd", ldphi, n=8, deps=[cst.ready])
        t_pb = pb.dma("sync", lambda e, sem: e.dma_start(out=phib_row[:], in_=W["phib_row"]).then_inc(sem, 16),
                      n=1, deps=[cst.ready])

        t_b1 = pb.comp("tensor", lambda t: (
            t.matmul(out=pss[0][:, :C], lhsT=cst.onesf_row[:], rhs=phib_row[0:1, 0:512], start=True, stop=True),
            t.matmul(out=pss[1][:, :C], lhsT=cst.onesf_row[:], rhs=phib_row[0:1, 512:1024], start=True, stop=True))[-1],
            deps=[t_pb])
        t_c1 = pb.comp("vector", lambda v: (
            v.tensor_copy(out=bphiB[:, 0:512], in_=pss[0][:, :C]),
            v.tensor_copy(out=bphiB[:, 512:1024], in_=pss[1][:, :C]))[-1], deps=[t_b1])
        t_b2 = pb.comp("tensor", lambda t: (
            t.matmul(out=pss[0][:, :C], lhsT=cst.onesf_row[:], rhs=phib_row[0:1, 1024:1536], start=True, stop=True),
            t.matmul(out=pss[1][:, :C], lhsT=cst.onesf_row[:], rhs=phib_row[0:1, 1536:2048], start=True, stop=True))[-1],
            deps=[t_c1])
        t_c2 = pb.comp("vector", lambda v: (
            v.tensor_copy(out=bphiB[:, 1024:1536], in_=pss[0][:, :C]),
            v.tensor_copy(out=bphiB[:, 1536:2048], in_=pss[1][:, :C]))[-1], deps=[t_b2])
        pb.state[("psum", id(pss[0]))] = [t_c2, t_c2]

        chain = {"soft": [None], "tr": [None], "ctdma": [None]}
        for ci in range(NCH):
            t0 = ci * C

            def ldx(e, sem, t0=t0):
                for k in range(8):
                    e.dma_start(out=Xc[k][:], in_=xin[k * 128:(k + 1) * 128, t0:t0 + C]).then_inc(sem, 16)
            t_x = pb.dma("sync", ldx, n=8, deps=[chain["soft"][0]])

            def ln_apply(k, psb0, psb1, dep):
                def vfn(v, k=k):
                    v.tensor_tensor(out=tA[:], in0=Xc[k][:], in1=psb0[:, :C], op=OP.subtract)
                    return v.tensor_tensor(out=XNb[k][:], in0=tA[:], in1=psb1[:, :C], op=OP.mult)
                return pb.comp("vector", vfn, deps=[dep])
            ln_toks, _ = emit_ln(pb, cst, lnsc, [x[:] for x in Xc], C, 8, ln_apply, xs_dep=t_x)

            st_p = pb.state[("psum", id(pss[0]))]
            for j in range(4):
                gt = 4 * ci + j
                def xtr(t, j=j):
                    last = None
                    for dt in range(8):
                        last = t.transpose(out=ps_T[:, dt * 128:(dt + 1) * 128],
                                           in_=XNb[dt][:, j * 128:(j + 1) * 128],
                                           identity=cst.identb[:])
                    return last
                t_xtr = pb.comp("tensor", xtr, deps=ln_toks + [chain["tr"][0]])
                t_xev = pb.comp("vector", lambda v: v.tensor_copy(out=XTst[:], in_=ps_T[:, 0:1024]),
                                deps=[t_xtr, chain["ctdma"][0]])
                chain["tr"][0] = t_xev

                def st_xt(e, sem, gt=gt):
                    e.dma_start(out=scr["xtk"][gt * 128:(gt + 1) * 128, :], in_=XTst[:]).then_inc(sem, 16)
                t_xtd = pb.dma("sync", st_xt, n=1, deps=[t_xev])
                chain["ctdma"][0] = t_xtd

                lg_toks = []
                for npc in range(4):
                    pw = npc % 2

                    def lmm(t, j=j, npc=npc, pw=pw):
                        last = None
                        for dt in range(8):
                            last = t.matmul(out=pss[pw][:, :C], lhsT=XNb[dt][:, j * 128:(j + 1) * 128],
                                            rhs=phiT[dt][:, npc * 512:(npc + 1) * 512],
                                            start=(dt == 0), stop=(dt == 7))
                        return last
                    t_lm = pb.comp("tensor", lmm, deps=[st_p[pw], t_phi] + ln_toks)

                    def lev(v, npc=npc, pw=pw):
                        return v.tensor_tensor(out=Lg[:, npc * 512:(npc + 1) * 512],
                                               in0=pss[pw][:, :C],
                                               in1=bphiB[:, npc * 512:(npc + 1) * 512], op=OP.add)
                    t_le = pb.comp("vector", lev, deps=[t_lm, chain["soft"][0]])
                    st_p[pw] = t_le
                    lg_toks.append(t_le)

                def red(v):
                    return v.tensor_reduce(out=nmax[:], in_=Lg[:], axis=AX.X, op=OP.max, negate=True)
                t_red = pb.comp("vector", red, deps=lg_toks)
                t_exp = pb.comp("scalar", lambda s: s.activation(out=Eg[:], in_=Lg[:], func=AF.Exp,
                                                                 bias=nmax[:]), deps=[t_red])

                def sums(v, sem, base):
                    v.tensor_reduce(out=sumD[:], in_=Eg[:].rearrange("p (g q) -> p g q", g=64),
                                    axis=AX.X, op=OP.add)
                    v.tensor_reduce(out=sumC[:], in_=Eg[:].rearrange("p (g q) -> p q g", g=64),
                                    axis=AX.X, op=OP.add)
                    v.reciprocal(out=sumD[:], in_=sumD[:])
                    v.reciprocal(out=sumC[:], in_=sumC[:]).then_inc(sem, 1)
                    v.wait_ge(sem, base + 1)
                    v.tensor_tensor(out=Db[:].rearrange("p (g q) -> p g q", g=64),
                                    in0=Eg[:].rearrange("p (g q) -> p g q", g=64),
                                    in1=sumD[:].unsqueeze(2).broadcast_to([128, 64, 32]),
                                    op=OP.mult)
                    v.tensor_tensor(out=Cb[:].rearrange("p (g q) -> p g q", g=64),
                                    in0=Eg[:].rearrange("p (g q) -> p g q", g=64),
                                    in1=sumC[:].unsqueeze(1).broadcast_to([128, 64, 32]),
                                    op=OP.mult).then_inc(sem, 1)
                t_soft = pb.step("vector", sums, 2, deps=[t_exp])
                chain["soft"][0] = t_soft

                def st_d(e, sem, gt=gt):
                    e.dma_start(out=scr["dscr"][gt * 128:(gt + 1) * 128, :], in_=Db[:]).then_inc(sem, 16)
                t_dd = pb.dma("sync", st_d, n=1, deps=[t_soft])

                for half in range(2):
                    def ctr(tt, half=half):
                        last = None
                        for i in range(8):
                            ii = half * 8 + i
                            last = tt.transpose(out=ps_T[:, i * 128:(i + 1) * 128],
                                                in_=Cb[:, ii * 128:(ii + 1) * 128],
                                                identity=cst.identb[:])
                        return last
                    t_ct = pb.comp("tensor", ctr, deps=[t_soft, chain["tr"][0]])

                    def cev(v, half=half, j=j):
                        last = None
                        for i in range(8):
                            last = v.tensor_copy(out=CTs[half * 8 + i][:, j * 128:(j + 1) * 128],
                                                 in_=ps_T[:, i * 128:(i + 1) * 128])
                        return last
                    t_ce = pb.comp("vector", cev, deps=[t_ct, chain["ctdma"][0]])
                    chain["tr"][0] = t_ce

            def st_ct(e, sem, ci=ci):
                for i in range(16):
                    e.dma_start(out=scr["cscr"][i * 128:(i + 1) * 128, ci * C:(ci + 1) * C],
                                in_=CTs[i][:]).then_inc(sem, 16)
            t_ctd = pb.dma("sync", st_ct, n=16, deps=[chain["tr"][0]])
            chain["ctdma"][0] = t_ctd

        pb.barrier()
        pb.es = old_es

    with ExitStack() as es2:
        old_es = pb.es
        pb.es = es2
        sb = lambda n, s, d=F32: pb.sbuf(tag + "p2_" + n, s, d)
        XTk = [sb(f"xtk{k}", [128, C], BF16) for k in range(32)]
        Dblk = [sb(f"dblk{p}", [128, 4096], BF16) for p in range(2)]
        stg = [sb(f"stg{p}", [128, C]) for p in range(2)]
        pss2 = [pb.psum(tag + "q0", [128, C]), pb.psum(tag + "q1", [128, C])]
        st2 = [None, None]
        ld_tok = [None, None]
        ev_tok = [None, None]
        dma_tok = [None, None]

        for j2 in range(2):
            def ldxt(e, sem, j2=j2):
                for k in range(32):
                    e.dma_start(out=XTk[k][:],
                                in_=scr["xtk"][k * 128:(k + 1) * 128, j2 * 512:(j2 + 1) * 512]).then_inc(sem, 16)
            t_xt = pb.dma("sync", ldxt, n=32, deps=[st2[0], st2[1]])
            for i in range(16):
                pw = i % 2

                def ldd(e, sem, i=i, pw=pw):
                    for g0 in (0, 16):
                        src = scr["dscr"][g0 * 128:(g0 + 16) * 128, i * 128:(i + 1) * 128].rearrange(
                            "(kt p) e -> p kt e", p=128)
                        dst = Dblk[pw][:, g0 * 128:(g0 + 16) * 128].rearrange("p (kt e) -> p kt e", kt=16)
                        e.dma_start(out=dst, in_=src).then_inc(sem, 16)
                t_ld = pb.dma("gpsimd", ldd, n=2, deps=[st2[pw]])
                ld_tok[pw] = t_ld

                def xmm(t, pw=pw):
                    last = None
                    for k in range(32):
                        last = t.matmul(out=pss2[pw][:, :C], lhsT=Dblk[pw][:, k * 128:(k + 1) * 128],
                                        rhs=XTk[k][:], start=(k == 0), stop=(k == 31))
                    return last
                t_mm = pb.comp("tensor", xmm, deps=[t_ld, t_xt, ev_tok[pw]])
                st2[pw] = t_mm

                t_ev = pb.comp("vector", lambda v, pw=pw: v.tensor_copy(out=stg[pw][:], in_=pss2[pw][:, :C]),
                               deps=[t_mm, dma_tok[pw]])
                ev_tok[pw] = t_ev

                def std(e, sem, i=i, j2=j2, pw=pw):
                    e.dma_start(out=scr["cc1_in"][i * 128:(i + 1) * 128, j2 * 512:(j2 + 1) * 512],
                                in_=stg[pw][:]).then_inc(sem, 16)
                t_sd = pb.dma("sync", std, n=1, deps=[t_ev])
                dma_tok[pw] = t_sd

        t_rs = pb.comp("gpsimd", lambda g: g.collective_compute(
            "ReduceScatter", mybir.AluOpType.add, replica_groups=RG,
            ins=[scr["cc1_in"]], outs=[scr["cc1_out"]]),
            deps=[dma_tok[0], dma_tok[1]])
        pb.barrier()
        pb.es = old_es

    with ExitStack() as es3:
        old_es = pb.es
        pb.es = es3
        sb = lambda n, s, d=F32: pb.sbuf(tag + "p3_" + n, s, d)
        xt2 = [sb(f"xt{k}", [128, 1024]) for k in range(2)]
        sq2 = sb("sq2", [128, 1024])
        s1 = sb("s1", [128, 1]); s2 = sb("s2", [128, 1]); s3 = sb("s3", [128, 1])
        XtN = [sb(f"xtn{k}", [128, 1024], BF16) for k in range(2)]
        XTl = [sb(f"xtl{k}", [128, 256], BF16) for k in range(8)]
        ewb = [sb(f"ewb{p}", [128, 4096], BF16) for p in range(2)]
        ebt = sb("ebt", [32, 1024])
        Yst = [sb(f"yst{k}", [128, 1024], BF16) for k in range(2)]
        ps3 = [pb.psum(tag + "r0", [128, C]), pb.psum(tag + "r1", [128, C])]
        ps_T3 = pb.psum(tag + "psT3", [128, 1024], BF16)

        def ldcc(e, sem):
            for k in range(2):
                e.dma_start(out=xt2[k][:], in_=scr["cc1_out"][k * 128:(k + 1) * 128, :]).then_inc(sem, 16)
        t_cc = pb.dma("sync", ldcc, n=2, deps=[t_rs])

        tr_toks = []
        for k in range(2):
            def pp(v, sem, base, k=k):
                v.tensor_scalar(out=xt2[k][:], in0=xt2[k][:], scalar1=-33000.0, scalar2=65000.0,
                                op0=OP.max, op1=OP.min)
                v.tensor_reduce(out=s1[:], in_=xt2[k][:], axis=AX.X, op=OP.add).then_inc(sem, 1)
                v.wait_ge(sem, base + 1)
                v.tensor_scalar(out=s1[:], in0=s1[:], scalar1=1.0 / 1024, scalar2=None,
                                op0=OP.mult, op1=OP.bypass).then_inc(sem, 1)
            t_p1 = pb.step("vector", pp, 2, deps=[t_cc])
            t_sq = pb.comp("scalar", lambda s, k=k: s.activation(out=sq2[:], in_=xt2[k][:],
                                                                 func=AF.Square), deps=[t_p1])

            def pp2(v, sem, base, k=k):
                v.tensor_reduce(out=s2[:], in_=sq2[:], axis=AX.X, op=OP.add).then_inc(sem, 1)
                v.wait_ge(sem, base + 1)
                v.tensor_scalar(out=s2[:], in0=s2[:], scalar1=1.0 / 1024, scalar2=None,
                                op0=OP.mult, op1=OP.bypass)
                v.tensor_tensor(out=s3[:], in0=s1[:], in1=s1[:], op=OP.mult)
                v.tensor_tensor(out=s2[:], in0=s2[:], in1=s3[:], op=OP.subtract).then_inc(sem, 1)
            t_p2 = pb.step("vector", pp2, 2, deps=[t_sq])

            def rstd(s, k=k):
                s.activation(out=s2[:], in_=s2[:], func=AF.Ln, bias=cst.cval(1e-5))
                return s.activation(out=s2[:], in_=s2[:], func=AF.Exp, scale=-0.5)
            t_p3 = pb.comp("scalar", rstd, deps=[t_p2])

            def nrm(v, k=k):
                v.tensor_scalar(out=xt2[k][:], in0=xt2[k][:], scalar1=s1[:], scalar2=None,
                                op0=OP.subtract, op1=OP.bypass)
                return v.tensor_scalar(out=xt2[k][:], in0=xt2[k][:], scalar1=s2[:], scalar2=None,
                                       op0=OP.mult, op1=OP.bypass)
            t_p4 = pb.comp("vector", nrm, deps=[t_p3])
            t_p5 = pb.comp("scalar", lambda s, k=k: s.activation(out=XtN[k][:], in_=xt2[k][:],
                                                                 func=AF.Tanh), deps=[t_p4])

            def xtr3(t, k=k):
                last = None
                for dt in range(8):
                    last = t.transpose(out=ps_T3[:, dt * 128:(dt + 1) * 128],
                                       in_=XtN[k][:, dt * 128:(dt + 1) * 128],
                                       identity=cst.identb[:])
                return last
            t_tr = pb.comp("tensor", xtr3, deps=[t_p5, tr_toks[-1] if tr_toks else None])

            def xtev(v, k=k):
                last = None
                for dt in range(8):
                    last = v.tensor_copy(out=XTl[dt][:, k * 128:(k + 1) * 128],
                                         in_=ps_T3[:, dt * 128:(dt + 1) * 128])
                return last
            t_te = pb.comp("vector", xtev, deps=[t_tr])
            tr_toks.append(t_te)

        ew_tok = [None, None]
        ps_tok = [None, None]
        yev_tok = [None, None]
        eb_tok = [None]
        ydma = [None]
        for eloc in range(8):
            def ldeb2(e, sem, eloc=eloc):
                e.dma_start(out=ebt[:], in_=W["ebR_my"][eloc * 32:(eloc + 1) * 32, :]).then_inc(sem, 16)
            t_eb = pb.dma("sync", ldeb2, n=1, deps=[eb_tok[0]])
            for hc in range(2):
                pw = (eloc * 2 + hc) % 2

                def ldew(e, sem, eloc=eloc, hc=hc, pw=pw):
                    e.dma_start(out=ewb[pw][:].rearrange("p (kt e) -> p kt e", kt=8),
                                in_=W["ew_my"][eloc, :, hc * 512:(hc + 1) * 512].rearrange(
                                    "(kt p) e -> p kt e", p=128)).then_inc(sem, 16)
                t_ew = pb.dma("gpsimd", ldew, n=1, deps=[ew_tok[pw]])
                ew_tok[pw] = t_ew

                def emm(t, eloc=eloc, hc=hc, pw=pw):
                    last = None
                    for dt in range(8):
                        last = t.matmul(out=ps3[pw][0:32, :C],
                                        lhsT=XTl[dt][:, eloc * 32:(eloc + 1) * 32],
                                        rhs=ewb[pw][:, dt * 512:(dt + 1) * 512],
                                        start=(dt == 0), stop=(dt == 7))
                    return last
                t_mm = pb.comp("tensor", emm, deps=[t_ew, ps_tok[pw], tr_toks[-1]])
                ew_tok[pw] = t_mm

                def yev(v, eloc=eloc, hc=hc, pw=pw):
                    return v.tensor_tensor(out=Yst[eloc // 4][(eloc % 4) * 32:(eloc % 4) * 32 + 32,
                                                             hc * 512:(hc + 1) * 512],
                                           in0=ps3[pw][0:32, :C], in1=ebt[0:32, hc * 512:(hc + 1) * 512],
                                           op=OP.add)
                t_ye = pb.comp("vector", yev, deps=[t_mm, t_eb, ydma[0]])
                ps_tok[pw] = t_ye
                yev_tok[pw] = t_ye
            eb_tok[0] = yev_tok[(eloc * 2 + 1) % 2]

        def st_y(e, sem):
            for k in range(2):
                e.dma_start(out=scr["cc2_in"][k * 128:(k + 1) * 128, :], in_=Yst[k][:]).then_inc(sem, 16)
        t_yd = pb.dma("sync", st_y, n=2, deps=[yev_tok[0], yev_tok[1]])

        t_ag = pb.comp("gpsimd", lambda g: g.collective_compute(
            "AllGather", mybir.AluOpType.bypass, replica_groups=RG,
            ins=[scr["cc2_in"]], outs=[scr["cc2_out"]]), deps=[t_yd])
        pb.barrier()
        pb.es = old_es

    with ExitStack() as es5:
        old_es = pb.es
        pb.es = es5
        sb = lambda n, s, d=F32: pb.sbuf(tag + "p5_" + n, s, d)
        Ytl = [sb(f"ytl{k}", [128, 1024], BF16) for k in range(16)]
        CTb = [sb(f"ctb{k}", [128, C], BF16) for k in range(16)]
        Xc5 = [sb(f"xc{k}", [128, C]) for k in range(8)]
        outb = [sb(f"outb{p}", [128, C]) for p in range(2)]
        ps5 = [pb.psum(tag + "s0", [128, C]), pb.psum(tag + "s1b", [128, C])]

        def ldyt(e, sem):
            for k in range(16):
                e.dma_start(out=Ytl[k][:], in_=scr["cc2_out"][k * 128:(k + 1) * 128, :]).then_inc(sem, 16)
        t_yt = pb.dma("gpsimd", ldyt, n=16, deps=[t_ag])

        mm_tok = [None, None]
        ev_tok5 = [None, None]
        dma5 = [None, None]
        ct_tok = [None]
        xq_tok = [None]
        for tc in range(NCH):
            t0 = tc * C

            def ldct(e, sem, tc=tc):
                for i in range(16):
                    e.dma_start(out=CTb[i][:],
                                in_=scr["cscr"][i * 128:(i + 1) * 128, tc * C:(tc + 1) * C]).then_inc(sem, 16)
            t_ct = pb.dma("sync", ldct, n=16, deps=[mm_tok[0], mm_tok[1]])

            def ldx5(e, sem, t0=t0):
                for k in range(8):
                    e.dma_start(out=Xc5[k][:], in_=xin[k * 128:(k + 1) * 128, t0:t0 + C]).then_inc(sem, 16)
            t_x5 = pb.dma("sync", ldx5, n=8, deps=[ev_tok5[0], ev_tok5[1]])

            for ht in range(8):
                pw = ht % 2

                def ymm(t, ht=ht, pw=pw):
                    last = None
                    for i in range(16):
                        last = t.matmul(out=ps5[pw][:, :C], lhsT=Ytl[i][:, ht * 128:(ht + 1) * 128],
                                        rhs=CTb[i][:], start=(i == 0), stop=(i == 15))
                    return last
                t_mm5 = pb.comp("tensor", ymm, deps=[t_ct, t_yt, ev_tok5[pw]])
                mm_tok[pw] = t_mm5

                def yev5(v, ht=ht, pw=pw):
                    return v.tensor_tensor(out=outb[pw][:], in0=ps5[pw][:, :C], in1=Xc5[ht][:], op=OP.add)
                t_ev5 = pb.comp("vector", yev5, deps=[t_mm5, t_x5, dma5[pw]])
                ev_tok5[pw] = t_ev5

                def st5(e, sem, ht=ht, pw=pw, t0=t0):
                    e.dma_start(out=xout[ht * 128:(ht + 1) * 128, t0:t0 + C], in_=outb[pw][:]).then_inc(sem, 16)
                t_d5 = pb.dma("sync", st5, n=1, deps=[t_ev5])
                dma5[pw] = t_d5
        pb.barrier()
        pb.es = old_es

    return pb.tok("sync")


def host_prep_l1_ffn(inp, rank):
    r = {}
    r["phiT"] = np.ascontiguousarray(inp["L1_ffn_phi_w"].T)
    r["phib_row"] = np.ascontiguousarray(inp["L1_ffn_phi_b"].reshape(1, 2048))
    r["ew_my"] = np.ascontiguousarray(inp["L1_ffn_expert_w"][rank * 8:(rank + 1) * 8])
    r["ebR_my"] = np.ascontiguousarray(
        np.repeat(inp["L1_ffn_expert_b"][rank * 8:(rank + 1) * 8], 32, axis=0))
    return r


L1F_WSPECS = [("phiT", [1024, 2048]), ("phib_row", [1, 2048]),
              ("ew_my", [8, 1024, 1024]), ("ebR_my", [256, 1024])]

L1F_SCRATCH = [("xtk", [4096, 1024], "bf16"), ("dscr", [4096, 2048], "bf16"),
               ("cscr", [2048, 4096], "bf16"), ("cc1_in", [2048, 1024], "f32"),
               ("cc1_out", [256, 1024], "f32"), ("cc2_in", [256, 1024], "bf16"),
               ("cc2_out", [2048, 1024], "bf16")]



def emit_l1_attn_a(pb, cst, W, xin, xout, scr):
    C = 512
    NCH = T // C
    RW = [0.2, 0.45, 0.15, 0.2]

    with ExitStack() as es0:
        old_es = pb.es
        pb.es = es0
        sb = lambda n, s, d=F32: pb.sbuf("l1a0_" + n, s, d)
        epiT = sb("epiT", [128, 4096])
        wdc1 = sb("wdc1", [128, 512], BF16)
        wdc2 = [sb(f"wdc2_{k}", [128, 1024], BF16) for k in range(4)]
        bdc1 = sb("bdc1", [128, 4])
        bdc2 = sb("bdc2", [128, 8])
        hbf = [sb(f"hbf{k}", [128, C], BF16) for k in range(4)]
        epb = sb("epb", [128, C], BF16)
        dstage = sb("dstage", [128, 1024], BF16)
        decfc = [sb(f"decfc{k}", [128, C], BF16) for k in range(8)]
        pss0 = [pb.psum("l1a0_p0", [128, C]), pb.psum("l1a0_p1", [128, C])]
        ps_T0 = pb.psum("l1a0_psT", [128, 1024], BF16)

        def ldp0(e, sem):
            e.dma_start(out=epiT[:], in_=W["epiT"]).then_inc(sem, 16)
            e.dma_start(out=bdc1[:], in_=W["bdc1_4"]).then_inc(sem, 16)
            e.dma_start(out=bdc2[:], in_=W["bdc2_8"]).then_inc(sem, 16)
        t_p0 = pb.dma("sync", ldp0, n=3, deps=[cst.ready])

        def ldw0(e, sem):
            e.dma_start(out=wdc1[:], in_=W["w_dc1"]).then_inc(sem, 16)
            for k in range(4):
                e.dma_start(out=wdc2[k][:], in_=W["w_dc2"][k * 128:(k + 1) * 128, :]).then_inc(sem, 16)
        t_w0 = pb.dma("gpsimd", ldw0, n=5, deps=[cst.ready])

        st_p0 = [None, None]
        prev_ep = [None]
        for sc_ in range(NCH):
            s0 = sc_ * C
            t_epb = pb.comp("vector", lambda v, s0=s0: v.tensor_copy(
                out=epb[:], in_=epiT[:, s0:s0 + C]), deps=[t_p0, prev_ep[0]])

            h_toks = []
            for m in range(4):
                pw = m % 2
                t_hm = pb.comp("tensor", lambda t, m=m, pw=pw: t.matmul(
                    out=pss0[pw][:, :C], lhsT=wdc1[:, m * 128:(m + 1) * 128],
                    rhs=epb[:], start=True, stop=True),
                    deps=[t_epb, t_w0, st_p0[pw]])
                t_he = pb.comp("scalar", lambda s, m=m, pw=pw: s.activation(
                    out=hbf[m][:], in_=pss0[pw][:, :C], func=AF.Gelu, bias=bdc1[:, m:m + 1]),
                    deps=[t_hm, t_p0])
                st_p0[pw] = t_he
                h_toks.append(t_he)

            d_toks = []
            for e8 in range(8):
                pw = e8 % 2

                def dmm(t, e8=e8, pw=pw):
                    last = None
                    for k in range(4):
                        last = t.matmul(out=pss0[pw][:, :C], lhsT=wdc2[k][:, e8 * 128:(e8 + 1) * 128],
                                        rhs=hbf[k][:], start=(k == 0), stop=(k == 3))
                    return last
                t_dm = pb.comp("tensor", dmm, deps=h_toks + [st_p0[pw]])
                t_de = pb.comp("vector", lambda v, e8=e8, pw=pw: v.tensor_scalar(
                    out=decfc[e8][:], in0=pss0[pw][:, :C], scalar1=bdc2[:, e8:e8 + 1],
                    scalar2=None, op0=OP.add, op1=OP.bypass), deps=[t_dm, t_p0, prev_ep[0]])
                st_p0[pw] = t_de
                d_toks.append(t_de)

            def st_df(e, sem, s0=s0):
                for e8 in range(8):
                    e.dma_start(out=scr["decf"][e8 * 128:(e8 + 1) * 128, s0:s0 + C],
                                in_=decfc[e8][:]).then_inc(sem, 16)
            t_sdf = pb.dma("sync", st_df, n=8, deps=d_toks)

            tr_prev = [t_sdf]
            for j in range(4):
                def dtr(t, j=j):
                    last = None
                    for e8 in range(8):
                        last = t.transpose(out=ps_T0[:, e8 * 128:(e8 + 1) * 128],
                                           in_=decfc[e8][:, j * 128:(j + 1) * 128],
                                           identity=cst.identb[:])
                    return last
                t_tr = pb.comp("tensor", dtr, deps=d_toks + [tr_prev[0]])
                t_te = pb.comp("vector", lambda v: v.tensor_copy(out=dstage[:], in_=ps_T0[:]),
                               deps=[t_tr, tr_prev[0]])

                def st_dt(e, sem, s0=s0, j=j):
                    e.dma_start(out=scr["dectok"][s0 + j * 128:s0 + (j + 1) * 128, :],
                                in_=dstage[:]).then_inc(sem, 16)
                tr_prev[0] = pb.dma("sync", st_dt, n=1, deps=[t_te])
            prev_ep[0] = tr_prev[0]
        pb.barrier()
        pb.es = old_es

    with ExitStack() as es1:
        old_es = pb.es
        pb.es = es1
        sb = lambda n, s, d=F32: pb.sbuf("l1a1_" + n, s, d)
        ACC = [sb(f"acc{k}", [128, C]) for k in range(8)]
        Xf = [sb(f"xf{k}", [128, C]) for k in range(8)]
        XNbb = [sb(f"xnbb{k}", [128, C], BF16) for k in range(8)]
        xpf = [sb(f"xpf{k}", [128, C]) for k in range(2)]
        xpb = sb("xpb", [128, C], BF16)
        delt = sb("delt", [128, 8 * C])
        wti = sb("wti", [128, 8 * C], BF16)
        abuf = sb("abuf", [128, 8 * C])
        XNf = [abuf[:, k * C:(k + 1) * C] for k in range(8)]
        ubuf = [sb(f"ubuf{p}", [128, 8 * C], BF16) for p in range(2)]
        hbuf = sb("hbuf", [128, 8 * C])
        Bb = [sb(f"Bb{p}", [128, C], BF16) for p in range(2)]
        Cb = [sb(f"Cb{p}", [128, C], BF16) for p in range(2)]
        ysum_all = sb("ysall", [128, 8 * C])
        ysum = [ysum_all[:, k * C:(k + 1) * C] for k in range(8)]
        ysb = [sb(f"ysb{k}", [128, C], BF16) for k in range(8)]
        carry = sb("carry", [128, 512])
        Aneg = sb("Aneg", [128, 512])
        wdt = sb("wdt", [64, 1024], BF16)
        cbol = [sb(f"cbol{p}", [128, 24]) for p in range(2)]
        XE = sb("XE", [128, C + 3])
        ctmp = sb("ctmp", [128, C])
        ctmp2 = sb("ctmp2", [128, C])
        wbufs = [sb(f"wbuf{p}", [128, 1024], BF16) for p in range(2)]
        P1 = {}
        for nm, w_ in [("lng8", 8), ("lnb8", 8), ("bdt8", 8), ("D8", 8), ("bwm8", 8),
                       ("wc32", 32), ("bcv8", 8)]:
            P1[nm] = sb("p_" + nm, [128, w_])
        lnps_a = pb.psum("l1a1_lnA", [128, C])
        lnps_b = pb.psum("l1a1_lnB", [128, C])
        lnsc = {
            "tb": [sb(f"lntb{p}", [128, C], BF16) for p in range(2)],
            "tq": [sb(f"lntq{p}", [128, C], BF16) for p in range(2)],
            "rows": [sb(f"lnrow{i}", [1, C]) for i in range(4)],
            "ps_s1": lnps_a, "ps_s2": lnps_b, "psb0": lnps_a, "psb1": lnps_b,
        }
        pss = [pb.psum("l1a1_pp0", [128, C]), pb.psum("l1a1_pp1", [128, C])]

        def ldp1(e, sem):
            for nm, src in [("lng8", "ln_g8"), ("lnb8", "ln_b8"), ("bdt8", "b_dt8"),
                            ("D8", "D8"), ("bwm8", "b_wm8"), ("wc32", "wc32"), ("bcv8", "b_conv8")]:
                e.dma_start(out=P1[nm][:], in_=W[src]).then_inc(sem, 16)
            e.dma_start(out=Aneg[:], in_=W["A8"]).then_inc(sem, 16)
        t_par = pb.dma("sync", ldp1, n=8, deps=[cst.ready])
        t_wdt = pb.dma("gpsimd", lambda e, sem: e.dma_start(out=wdt[:], in_=W["w_dt"]).then_inc(sem, 16),
                       n=1, deps=[cst.ready])
        t_a1 = pb.comp("scalar", lambda s: s.activation(out=Aneg[:], in_=Aneg[:], func=AF.Exp),
                       deps=[t_par])
        t_a2 = pb.comp("vector", lambda v: v.tensor_scalar(
            out=Aneg[:], in0=Aneg[:], scalar1=-1.0, scalar2=None, op0=OP.mult, op1=OP.bypass),
            deps=[t_a1])

        s1 = {"ssm": [None], "acc_st": [None] * 8, "xb_st": [None], "row_ld": [None, None],
              "bc_st": [None]}

        for ci in range(NCH):
            t0 = ci * C
            par = ci % 2

            def ldx(e, sem, t0=t0):
                for k in range(8):
                    e.dma_start(out=ACC[k][:], in_=xin[k * 128:(k + 1) * 128, t0:t0 + C]).then_inc(sem, 16)
            t_x = pb.dma("sync", ldx, n=8, deps=s1["acc_st"])

            def lnt_apply(k, psb0, psb1, dep):
                def vfn(v, k=k):
                    v.tensor_tensor(out=ctmp[:], in0=ACC[k][:], in1=psb0[:, :C], op=OP.subtract)
                    return v.tensor_tensor(out=Xf[k][:], in0=ctmp[:], in1=psb1[:, :C], op=OP.mult)
                return pb.comp("vector", vfn, deps=[dep, s1["ssm"][0]])
            lnt_toks, _ = emit_ln(pb, cst, lnsc, [x[:] for x in ACC], C, 8, lnt_apply, xs_dep=t_x)

            for k in range(8):
                def st_xa(e, sem, k=k, t0=t0):
                    e.dma_start(out=scr["xa"][k * 128:(k + 1) * 128, t0:t0 + C],
                                in_=Xf[k][:]).then_inc(sem, 16)
                s1["xb_st"][0] = pb.dma("sync", st_xa, n=1, deps=[lnt_toks[k]])

            def ln1_apply(k, psb0, psb1, dep):
                def vfn(v, k=k):
                    v.tensor_tensor(out=ctmp[:], in0=Xf[k][:], in1=psb0[:, :C], op=OP.subtract)
                    v.tensor_tensor(out=ctmp2[:], in0=ctmp[:], in1=psb1[:, :C], op=OP.mult)
                    v.tensor_scalar(out=XNf[k], in0=ctmp2[:], scalar1=P1["lng8"][:, k:k + 1],
                                    scalar2=P1["lnb8"][:, k:k + 1], op0=OP.mult, op1=OP.add)
                    return v.tensor_copy(out=XNbb[k][:], in_=XNf[k])
                return pb.comp("vector", vfn, deps=[dep, t_par, s1["ssm"][0]])
            ln1_toks, _ = emit_ln(pb, cst, lnsc, [x[:] for x in Xf], C, 8, ln1_apply,
                                  xs_dep=lnt_toks[-1])

            def xp_evac(i, ps, mmtok):
                def vfn(v, i=i):
                    v.tensor_copy(out=xpf[i][:], in_=ps[:, :C])
                    if i == 0:
                        return v.tensor_copy(out=xpb[:], in_=ps[:, :C])
                    return v.tensor_copy(out=xpf[i][:], in_=ps[:, :C])
                return pb.comp("vector", vfn, deps=[mmtok, s1["ssm"][0], s1["bc_st"][0]])
            xp_toks = stream_matmul(pb, W["w_xprojP"], 8, 2, [x[:] for x in XNbb], C,
                                    wbufs, pss, xp_evac, rhs_deps=ln1_toks)

            def st_bc(e, sem, t0=t0):
                e.dma_start(out=scr["bcr"][0:64, t0:t0 + C], in_=xpf[0][64:128, :]).then_inc(sem, 16)
                e.dma_start(out=scr["bcr"][64:128, t0:t0 + C], in_=xpf[1][0:64, :]).then_inc(sem, 16)
            t_bcst = pb.dma("gpsimd", st_bc, n=2, deps=xp_toks)
            s1["bc_st"][0] = t_bcst

            dl_toks = []
            st_pp = pb.state[("psum", id(pss[0]))]
            for dt in range(8):
                pw = dt % 2
                t_dm = pb.comp("tensor", lambda t, dt=dt, pw=pw: t.matmul(
                    out=pss[pw][:, :C], lhsT=wdt[:, dt * 128:(dt + 1) * 128],
                    rhs=xpb[0:64, :], start=True, stop=True),
                    deps=[xp_toks[0], t_wdt, st_pp[pw]])

                def sfp(s, dt=dt, pw=pw):
                    s.activation(out=ctmp[:], in_=pss[pw][:, :C], func=AF.Abs,
                                 bias=P1["bdt8"][:, dt:dt + 1])
                    s.activation(out=ctmp[:], in_=ctmp[:], func=AF.Exp, scale=-1.0)
                    s.activation(out=ctmp[:], in_=ctmp[:], func=AF.Ln, bias=cst.cval(1.0))
                    return s.activation(out=ctmp2[:], in_=pss[pw][:, :C], func=AF.Relu,
                                        bias=P1["bdt8"][:, dt:dt + 1])
                t_sf = pb.comp("scalar", sfp, deps=[t_dm, t_par, s1["ssm"][0],
                                                    dl_toks[-1] if dl_toks else None])

                def dfin(v, dt=dt):
                    v.tensor_tensor(out=delt[:, dt * C:(dt + 1) * C], in0=ctmp[:], in1=ctmp2[:],
                                    op=OP.add)
                    return v.tensor_tensor(out=wti[:, dt * C:(dt + 1) * C],
                                           in0=delt[:, dt * C:(dt + 1) * C], in1=XNf[dt],
                                           op=OP.mult)
                t_df = pb.comp("vector", dfin, deps=[t_sf, ln1_toks[dt]])
                st_pp[pw] = t_df
                dl_toks.append(t_df)

            y0_tok = pb.comp("vector", lambda v: [v.memset(ysum[k], 0.0) for k in range(8)][-1],
                             deps=[s1["ssm"][0]])
            prev_y = [y0_tok]
            prev_a = [None]
            prev_ya = [y0_tok]
            for n in range(64):
                pr = n % 2

                def ldrow(e, sem, n=n, pr=pr, t0=t0):
                    e.dma_start(out=Bb[pr][:],
                                in_=scr["bcr"][n:n + 1, t0:t0 + C].broadcast_to([128, C])
                                ).then_inc(sem, 16)
                    e.dma_start(out=Cb[pr][:],
                                in_=scr["bcr"][64 + n:65 + n, t0:t0 + C].broadcast_to([128, C])
                                ).then_inc(sem, 16)
                t_lr = pb.dma("sync", ldrow, n=2, deps=[t_bcst, s1["row_ld"][pr]])

                t_u = pb.comp("vector", lambda v, n=n, pr=pr: v.tensor_tensor(
                    out=ubuf[pr][:].rearrange("p (dt c) -> p dt c", dt=8),
                    in0=wti[:].rearrange("p (dt c) -> p dt c", dt=8),
                    in1=Bb[pr][:].unsqueeze(1).broadcast_to([128, 8, C]), op=OP.mult),
                    deps=[t_lr] + dl_toks + [prev_ya[0]])

                def acomp(s, n=n):
                    last = None
                    for dt in range(8):
                        last = s.activation(out=abuf[:, dt * C:(dt + 1) * C],
                                            in_=delt[:, dt * C:(dt + 1) * C], func=AF.Exp,
                                            scale=Aneg[:, dt * 64 + n:dt * 64 + n + 1])
                    return last
                t_aa = pb.comp("scalar", acomp, deps=[t_a2, prev_a[0]] + dl_toks)

                def scomp(v, sem, base, n=n, ci=ci, pr=pr):
                    last = None
                    for dt in range(8):
                        init = 0.0 if ci == 0 else carry[:, n * 8 + dt:n * 8 + dt + 1]
                        last = v.tensor_tensor_scan(out=hbuf[:, dt * C:(dt + 1) * C],
                                                    data0=abuf[:, dt * C:(dt + 1) * C],
                                                    data1=ubuf[pr][:, dt * C:(dt + 1) * C],
                                                    initial=init, op0=OP.mult, op1=OP.add)
                    last.then_inc(sem, 1)
                    v.wait_ge(sem, base + 1)
                    v.tensor_copy(out=carry[:, n * 8:(n + 1) * 8].unsqueeze(2),
                                  in_=hbuf[:].rearrange("p (dt c) -> p dt c", dt=8)[:, :, C - 1:C]
                                  ).then_inc(sem, 1)
                t_sc = pb.step("vector", scomp, 2, deps=[t_u, t_aa])
                prev_a[0] = t_sc

                t_y = pb.comp("vector", lambda v, n=n, pr=pr: v.tensor_tensor(
                    out=ubuf[pr][:].rearrange("p (dt c) -> p dt c", dt=8),
                    in0=hbuf[:].rearrange("p (dt c) -> p dt c", dt=8),
                    in1=Cb[pr][:].unsqueeze(1).broadcast_to([128, 8, C]), op=OP.mult),
                    deps=[t_sc])
                prev_y[0] = t_y
                s1["row_ld"][pr] = t_y

                t_ya = pb.comp("gpsimd", lambda g, pr=pr: g.tensor_tensor(
                    out=ysum_all[:], in0=ysum_all[:], in1=ubuf[pr][:], op=OP.add),
                    deps=[t_y, prev_ya[0]])
                prev_ya[0] = t_ya
            s1["ssm"][0] = prev_ya[0]

            ysb_toks = []
            for dt in range(8):
                def ysf(v, dt=dt):
                    v.tensor_scalar(out=ctmp[:], in0=Xf[dt][:], scalar1=P1["D8"][:, dt:dt + 1],
                                    scalar2=None, op0=OP.mult, op1=OP.bypass)
                    v.tensor_tensor(out=ctmp[:], in0=ctmp[:], in1=ysum[dt], op=OP.add)
                    return v.tensor_copy(out=ysb[dt][:], in_=ctmp[:])
                ysb_toks.append(pb.comp("vector", ysf, deps=[prev_y[0], t_par]))

            acc0_toks = []
            for k in range(8):
                acc0_toks.append(pb.comp("vector", lambda v, k=k: v.tensor_tensor(
                    out=ACC[k][:], in0=ACC[k][:], in1=Xf[k][:], op=OP.add), deps=[lnt_toks[k]]))

            def ssm_evac(i, ps, mmtok):
                return pb.comp("vector", lambda v, i=i: v.scalar_tensor_tensor(
                    out=ACC[i][:], in0=ps[:, :C], scalar=RW[1], in1=ACC[i][:],
                    op0=OP.mult, op1=OP.add), deps=[mmtok, acc0_toks[i]])
            ssm_toks = stream_matmul(pb, W["w_ssm_out"], 8, 8, [x[:] for x in ysb], C,
                                     wbufs, pss, ssm_evac, rhs_deps=ysb_toks)

            xfb_toks = []
            for k in range(8):
                t_c = pb.comp("vector", lambda v, k=k: v.tensor_copy(out=XNbb[k][:], in_=Xf[k][:]),
                              deps=[lnt_toks[k]] + xp_toks + dl_toks)

                def st_xb(e, sem, k=k, t0=t0):
                    e.dma_start(out=scr["xb_scr"][k * 128:(k + 1) * 128, t0:t0 + C],
                                in_=XNbb[k][:]).then_inc(sem, 16)
                pb.dma("sync", st_xb, n=1, deps=[t_c])
                xfb_toks.append(t_c)

            def wm_evac(i, ps, mmtok):
                def vfn(v, i=i):
                    v.tensor_scalar(out=ctmp[:], in0=ps[:, :C], scalar1=P1["bwm8"][:, i:i + 1],
                                    scalar2=None, op0=OP.add, op1=OP.bypass)
                    return v.scalar_tensor_tensor(out=ACC[i][:], in0=ctmp[:], scalar=RW[3],
                                                  in1=ACC[i][:], op0=OP.mult, op1=OP.add)
                return pb.comp("vector", vfn, deps=[mmtok, ssm_toks[i], t_par])
            wm_toks = stream_matmul(pb, W["w_wm"], 8, 8, [x[:] for x in XNbb], C,
                                    wbufs, pss, wm_evac, rhs_deps=xfb_toks)

            cv_toks = []
            for dt in range(8):
                def cvf(v, dt=dt, ci=ci, par=par):
                    if ci == 0:
                        v.memset(XE[:, 0:3], 0.0)
                    else:
                        v.tensor_copy(out=XE[:, 0:3], in_=cbol[1 - par][:, dt * 3:dt * 3 + 3])
                    v.tensor_copy(out=XE[:, 3:C + 3], in_=Xf[dt][:])
                    v.tensor_copy(out=cbol[par][:, dt * 3:dt * 3 + 3], in_=Xf[dt][:, C - 3:C])
                    v.tensor_scalar(out=ctmp[:], in0=XE[:, 3:C + 3],
                                    scalar1=P1["wc32"][:, dt * 4 + 3:dt * 4 + 4], scalar2=None,
                                    op0=OP.mult, op1=OP.bypass)
                    for i in range(3):
                        v.scalar_tensor_tensor(out=ctmp[:], in0=XE[:, i:i + C],
                                               scalar=P1["wc32"][:, dt * 4 + i:dt * 4 + i + 1],
                                               in1=ctmp[:], op0=OP.mult, op1=OP.add)
                    return v.tensor_copy(out=ctmp2[:], in_=ctmp[:])
                t_cv = pb.comp("vector", cvf, deps=[lnt_toks[dt], wm_toks[dt],
                                                    cv_toks[-1] if cv_toks else None])
                t_si = pb.comp("scalar", lambda s, dt=dt: s.activation(
                    out=ctmp2[:], in_=ctmp2[:], func=AF.Silu, bias=P1["bcv8"][:, dt:dt + 1]),
                    deps=[t_cv])
                t_ca = pb.comp("vector", lambda v, dt=dt: v.scalar_tensor_tensor(
                    out=ACC[dt][:], in0=ctmp2[:], scalar=RW[2], in1=ACC[dt][:],
                    op0=OP.mult, op1=OP.add), deps=[t_si])
                cv_toks.append(t_ca)

            for k in range(8):
                def st_acc(e, sem, k=k, t0=t0):
                    e.dma_start(out=scr["acc"][k * 128:(k + 1) * 128, t0:t0 + C],
                                in_=ACC[k][:]).then_inc(sem, 16)
                s1["acc_st"][k] = pb.dma("sync", st_acc, n=1, deps=[cv_toks[k]])
            pb.barrier()

        pb.barrier()
        pb.es = old_es

    with ExitStack() as es2:
        old_es = pb.es
        pb.es = es2
        sb = lambda n, s, d=F32: pb.sbuf("l1a2_" + n, s, d)
        Xb = [sb(f"xb{k}", [128, 4096], BF16) for k in range(8)]
        Kf = [sb(f"kf{k}", [128, 4096], BF16) for k in range(2)]
        Vt = [sb(f"vt{k}", [128, 256], BF16) for k in range(32)]
        Wv = [sb(f"wv{k}", [128, 256], BF16) for k in range(8)]
        Qf = [sb(f"qf{k}", [128, C], BF16) for k in range(8)]
        AOf = [sb(f"ao{k}", [128, C], BF16) for k in range(8)]
        RBM = [sb(f"rbm{h}", [128, 640], BF16) for h in range(16)]
        ACC2 = [sb(f"acc{k}", [128, C]) for k in range(8)]
        Sm = [sb(f"Sm{p}", [128, 640]) for p in range(2)]
        Se = [sb(f"Se{p}", [128, 640]) for p in range(2)]
        Sn = [sb(f"Sn{p}", [128, 640], BF16) for p in range(2)]
        STt = [sb(f"ST{p}", [128, 640], BF16) for p in range(2)]
        nmax = [sb(f"nmax{p}", [128, 1]) for p in range(2)]
        rsum = [sb(f"rsum{p}", [128, 1]) for p in range(2)]
        outb = [sb(f"outb{p}", [128, C]) for p in range(2)]
        wbufs = [sb(f"wbuf{p}", [128, 1024], BF16) for p in range(2)]
        pss = [pb.psum("l1a2_pp0", [128, C]), pb.psum("l1a2_pp1", [128, C])]
        ps_S = pb.psum("l1a2_psS", [128, 640])
        ps_T = pb.psum("l1a2_psT", [128, 640], BF16)

        def ldxb(e, sem):
            for k in range(8):
                e.dma_start(out=Xb[k][:], in_=scr["xb_scr"][k * 128:(k + 1) * 128, :]).then_inc(sem, 16)
        t_xb = pb.dma("sync", ldxb, n=8)

        def ldrbm(e, sem):
            for h in range(16):
                e.dma_start(out=RBM[h][:], in_=W["relbm"][h * 128:(h + 1) * 128, :]).then_inc(sem, 16)
        t_rbm = pb.dma("gpsimd", ldrbm, n=16, deps=[cst.ready])

        def ldwv(e, sem):
            for k in range(8):
                e.dma_start(out=Wv[k][:], in_=W["wv"][k * 128:(k + 1) * 128, :]).then_inc(sem, 16)
        t_wv = pb.dma("gpsimd", ldwv, n=8)

        k_toks = []
        for tc in range(NCH):
            def k_evac(i, ps, mmtok, tc=tc):
                return pb.comp("vector", lambda v, i=i, tc=tc: v.tensor_copy(
                    out=Kf[i][:, tc * C:(tc + 1) * C], in_=ps[:, :C]), deps=[mmtok])
            k_toks += stream_matmul(pb, W["wk"], 8, 2, [x[:, tc * C:(tc + 1) * C] for x in Xb],
                                    C, wbufs, pss, k_evac, rhs_deps=[t_xb])
        v_toks = []
        st_pp = pb.state[("psum", id(pss[0]))]
        for g in range(32):
            pw = g % 2

            def vmm(t, g=g, pw=pw):
                last = None
                for dtl in range(8):
                    last = t.matmul(out=pss[pw][:, 0:256], lhsT=Xb[dtl][:, g * 128:(g + 1) * 128],
                                    rhs=Wv[dtl][:], start=(dtl == 0), stop=(dtl == 7))
                return last
            t_vm = pb.comp("tensor", vmm, deps=[st_pp[pw], t_wv, t_xb])
            t_ve = pb.comp("vector", lambda v, g=g, pw=pw: v.tensor_copy(
                out=Vt[g][:], in_=pss[pw][:, 0:256]), deps=[t_vm])
            st_pp[pw] = t_ve
            v_toks.append(t_ve)

        ch = {k: [None, None] for k in ["stt", "exp", "norm", "tr", "stev", "av", "aoev"]}
        wo_mm = [None]
        dma_acc = [None, None]
        q_chunk_tok = [None]
        it = 0
        for ci in range(NCH):
            t0 = ci * C

            def q_evac(i, ps, mmtok):
                return pb.comp("vector", lambda v, i=i: v.tensor_scalar(
                    out=Qf[i][:], in0=ps[:, :C], scalar1=0.125, scalar2=None,
                    op0=OP.mult, op1=OP.bypass), deps=[mmtok, q_chunk_tok[0]])
            q_toks = stream_matmul(pb, W["wq"], 8, 8, [x[:, t0:t0 + C] for x in Xb], C,
                                   wbufs, pss, q_evac, rhs_deps=[t_xb])

            def ldacc(e, sem, t0=t0):
                for k in range(8):
                    e.dma_start(out=ACC2[k][:], in_=scr["acc"][k * 128:(k + 1) * 128,
                                                               t0:t0 + C]).then_inc(sem, 16)
            t_la = pb.dma("sync", ldacc, n=8, deps=[dma_acc[0], dma_acc[1]] + s1["acc_st"])

            QORDER = [0, 4, 1, 5, 2, 6, 3, 7, 8, 12, 9, 13, 10, 14, 11, 15]
            for j in range(4):
                tq = 4 * ci + j
                r0 = max(-2, -tq)
                r1 = min(2, 31 - tq)
                nk = r1 - r0 + 1
                for hp in range(16):
                    h = QORDER[hp]
                    p = it % 2
                    e_t = hp // 2
                    po = (hp % 2) * 64
                    kvh = h // 4
                    kt = kvh // 2
                    kpo = (kvh % 2) * 64
                    assert kpo == po

                    def qk(t, j=j, h=h, nk=nk, r0=r0, tq=tq, e_t=e_t, po=po, kt=kt, kpo=kpo):
                        last = None
                        for ii in range(nk):
                            gk = tq + r0 + ii
                            last = t.matmul(out=ps_S[:, ii * 128:(ii + 1) * 128],
                                            lhsT=Qf[e_t][po:po + 64, j * 128:(j + 1) * 128],
                                            rhs=Kf[kt][kpo:kpo + 64, gk * 128:(gk + 1) * 128],
                                            start=True, stop=True)
                        return last
                    t_qk = pb.comp("tensor", qk, deps=[q_toks[e_t], ch["stt"][0], ch["stt"][1]]
                                   + k_toks)

                    def sttm(v, p=p, nk=nk, r0=r0, h=h):
                        c0 = (r0 + 2) * 128
                        return v.scalar_tensor_tensor(
                            out=Sm[p][:, :nk * 128], in0=ps_S[:, :nk * 128], scalar=1.0,
                            in1=RBM[h][:, c0:c0 + nk * 128], op0=OP.mult, op1=OP.add)
                    t_stt = pb.comp("vector", sttm, deps=[t_qk, t_rbm, ch["exp"][0], ch["exp"][1]])
                    ch["stt"][p] = t_stt

                    t_red = pb.comp("vector", lambda v, p=p, nk=nk: v.tensor_reduce(
                        out=nmax[p][:], in_=Sm[p][:, :nk * 128], axis=AX.X, op=OP.max, negate=True),
                        deps=[t_stt])
                    t_exp = pb.comp("scalar", lambda s, p=p, nk=nk: s.activation(
                        out=Se[p][:, :nk * 128], in_=Sm[p][:, :nk * 128], func=AF.Exp,
                        bias=nmax[p][:], accum_out=rsum[p][:]), deps=[t_red, ch["norm"][p]])
                    ch["exp"][p] = t_exp

                    def norm(v, sem, base, p=p, nk=nk):
                        v.reciprocal(out=rsum[p][:], in_=rsum[p][:]).then_inc(sem, 1)
                        v.wait_ge(sem, base + 1)
                        v.tensor_scalar(out=Sn[p][:, :nk * 128], in0=Se[p][:, :nk * 128],
                                        scalar1=rsum[p][:], scalar2=None, op0=OP.mult,
                                        op1=OP.bypass).then_inc(sem, 1)
                    t_norm = pb.step("vector", norm, 2, deps=[t_exp, ch["tr"][p]])
                    ch["norm"][p] = t_norm

                    def trs(t, p=p, nk=nk):
                        last = None
                        for ii in range(nk):
                            last = t.transpose(out=ps_T[:, ii * 128:(ii + 1) * 128],
                                               in_=Sn[p][:, ii * 128:(ii + 1) * 128],
                                               identity=cst.identb[:])
                        return last
                    t_tr = pb.comp("tensor", trs, deps=[t_norm, ch["stev"][0], ch["stev"][1]])
                    ch["tr"][p] = t_tr

                    t_stev = pb.comp("vector", lambda v, p=p, nk=nk: v.tensor_copy(
                        out=STt[p][:, :nk * 128], in_=ps_T[:, :nk * 128]),
                        deps=[t_tr, ch["av"][p]])
                    ch["stev"][p] = t_stev

                    pw = it % 2

                    def av(t, p=p, nk=nk, r0=r0, tq=tq, kvh=kvh, pw=pw):
                        last = None
                        for ii in range(nk):
                            gk = tq + r0 + ii
                            last = t.matmul(out=pss[pw][0:64, 0:128],
                                            lhsT=Vt[gk][:, kvh * 64:(kvh + 1) * 64],
                                            rhs=STt[p][:, ii * 128:(ii + 1) * 128],
                                            start=(ii == 0), stop=(ii == nk - 1))
                        return last
                    t_av = pb.comp("tensor", av, deps=[t_stev, st_pp[pw]] + v_toks)
                    ch["av"][p] = t_av

                    t_ao = pb.comp("vector", lambda v, h=h, j=j, pw=pw, e_t=e_t, po=po: v.tensor_copy(
                        out=AOf[e_t][po:po + 64, j * 128:(j + 1) * 128], in_=pss[pw][0:64, 0:128]),
                        deps=[t_av, wo_mm[0]])
                    st_pp[pw] = t_ao
                    ch["aoev"][p] = t_ao
                    it += 1

            def wo_evac(i, ps, mmtok, t0=t0):
                p = i % 2
                t_v = pb.comp("vector", lambda v, i=i, p=p: v.scalar_tensor_tensor(
                    out=outb[p][:], in0=ps[:, :C], scalar=RW[0], in1=ACC2[i][:],
                    op0=OP.mult, op1=OP.add), deps=[mmtok, t_la, dma_acc[p]])

                def st(e, sem, i=i, p=p, t0=t0):
                    e.dma_start(out=scr["acc"][i * 128:(i + 1) * 128, t0:t0 + C],
                                in_=outb[p][:]).then_inc(sem, 16)
                t_d = pb.dma("sync", st, n=1, deps=[t_v])
                dma_acc[p] = t_d
                return t_v
            wo_toks = stream_matmul(pb, W["wo"], 8, 8, [x[:] for x in AOf], C, wbufs, pss,
                                    wo_evac, rhs_deps=[ch["aoev"][0], ch["aoev"][1]])
            wo_mm[0] = pb.state[("wbuf", id(wbufs[0]))][1]
            q_chunk_tok[0] = wo_mm[0]
            pb.barrier()
        pb.barrier()
        pb.es = old_es

    with ExitStack() as es3:
        old_es = pb.es
        pb.es = es3
        sb = lambda n, s, d=F32: pb.sbuf("l1a3_" + n, s, d)
        Xbq = [sb(f"xbq{k}", [128, C], BF16) for k in range(8)]
        dfb = [sb(f"dfb{p}", [128, 4096], BF16) for p in range(2)]
        dtb = [sb(f"dtb{p}", [128, 4096], BF16) for p in range(2)]
        Srow = [sb(f"srow{q}", [128, 4096]) for q in range(4)]
        Snb3 = sb("snb3", [128, 4096], BF16)
        STe = [sb(f"ste{k}", [128, C], BF16) for k in range(32)]
        nmax3 = sb("nmax3", [128, 4])
        rsum3 = sb("rsum3", [128, 4])
        ACC3 = [sb(f"acc{k}", [128, C]) for k in range(8)]
        outb3 = [sb(f"outb{p}", [128, C]) for p in range(2)]
        ps3 = [pb.psum("l1a3_p0", [128, C]), pb.psum("l1a3_p1", [128, C])]
        ps_T3 = pb.psum("l1a3_psT", [128, C], BF16)

        st3 = [None, None]
        xbq_ld = [None]
        df_ld = [None, None]
        dt_ld = [None, None]
        tr3 = [None]
        ste_use = [None]
        dma_out3 = [None, None]
        acc_ld3 = [None]
        for ci in range(NCH):
            t0 = ci * C

            def ldxbq(e, sem, t0=t0):
                for k in range(8):
                    e.dma_start(out=Xbq[k][:], in_=scr["xb_scr"][k * 128:(k + 1) * 128,
                                                                t0:t0 + C]).then_inc(sem, 16)
            t_xb3 = pb.dma("sync", ldxbq, n=8, deps=[xbq_ld[0]])
            sc_toks = []
            for sc_ in range(NCH):
                pr = sc_ % 2

                def lddf(e, sem, sc_=sc_, pr=pr):
                    for dtl in range(8):
                        e.dma_start(out=dfb[pr][:, dtl * 512:(dtl + 1) * 512],
                                    in_=scr["decf"][dtl * 128:(dtl + 1) * 128,
                                                    sc_ * C:(sc_ + 1) * C]).then_inc(sem, 16)
                t_df = pb.dma("gpsimd", lddf, n=8, deps=[df_ld[pr]])

                for j in range(4):
                    pw = (sc_ * 4 + j) % 2

                    def smm(t, j=j, pr=pr, pw=pw):
                        last = None
                        for dtl in range(8):
                            last = t.matmul(out=ps3[pw][:, :C],
                                            lhsT=Xbq[dtl][:, j * 128:(j + 1) * 128],
                                            rhs=dfb[pr][:, dtl * 512:(dtl + 1) * 512],
                                            start=(dtl == 0), stop=(dtl == 7))
                        return last
                    t_sm = pb.comp("tensor", smm, deps=[t_df, t_xb3, st3[pw]])
                    df_ld[pr] = t_sm
                    t_sev = pb.comp("vector", lambda v, j=j, sc_=sc_, pw=pw: v.tensor_copy(
                        out=Srow[j][:, sc_ * C:(sc_ + 1) * C], in_=ps3[pw][:, :C]),
                        deps=[t_sm, ste_use[0], tr3[0]])
                    st3[pw] = t_sev
                    sc_toks.append(t_sev)

            for j in range(4):
                t_r3 = pb.comp("vector", lambda v, j=j: v.tensor_reduce(
                    out=nmax3[:, j:j + 1], in_=Srow[j][:], axis=AX.X, op=OP.max, negate=True),
                    deps=sc_toks)

                def nsc(v, j=j):
                    return v.tensor_scalar(out=nmax3[:, j:j + 1], in0=nmax3[:, j:j + 1],
                                           scalar1=1.0 / 32.0, scalar2=None, op0=OP.mult,
                                           op1=OP.bypass)
                t_n3 = pb.comp("vector", nsc, deps=[t_r3])
                t_e3 = pb.comp("scalar", lambda s, j=j: s.activation(
                    out=Srow[j][:], in_=Srow[j][:], func=AF.Exp, scale=1.0 / 32.0,
                    bias=nmax3[:, j:j + 1], accum_out=rsum3[:, j:j + 1]), deps=[t_n3])

                def nrm3(v, sem, base, j=j):
                    v.reciprocal(out=rsum3[:, j:j + 1], in_=rsum3[:, j:j + 1]).then_inc(sem, 1)
                    v.wait_ge(sem, base + 1)
                    v.tensor_scalar(out=Snb3[:], in0=Srow[j][:], scalar1=rsum3[:, j:j + 1],
                                    scalar2=None, op0=OP.mult, op1=OP.bypass).then_inc(sem, 1)
                t_nm3 = pb.step("vector", nrm3, 2, deps=[t_e3, tr3[0]])

                for st_ in range(32):
                    t_t3 = pb.comp("tensor", lambda t, j=j, st_=st_: t.transpose(
                        out=ps_T3[:, 0:128], in_=Snb3[:, st_ * 128:(st_ + 1) * 128],
                        identity=cst.identb[:]), deps=[t_nm3, tr3[0]])
                    t_te3 = pb.comp("vector", lambda v, j=j, st_=st_: v.tensor_copy(
                        out=STe[st_][:, j * 128:(j + 1) * 128], in_=ps_T3[:, 0:128]),
                        deps=[t_t3, ste_use[0]])
                    tr3[0] = t_te3

            def ldacc3(e, sem, t0=t0):
                for k in range(8):
                    e.dma_start(out=ACC3[k][:], in_=scr["acc"][k * 128:(k + 1) * 128,
                                                               t0:t0 + C]).then_inc(sem, 16)
            t_la3 = pb.dma("sync", ldacc3, n=8, deps=[dma_out3[0], dma_out3[1], acc_ld3[0]])
            acc_ld3[0] = t_la3

            for e8 in range(8):
                pr = e8 % 2

                def lddt(e, sem, e8=e8, pr=pr):
                    for g0 in (0, 16):
                        src = scr["dectok"][g0 * 128:(g0 + 16) * 128,
                                            e8 * 128:(e8 + 1) * 128].rearrange(
                            "(kt p) e -> p kt e", p=128)
                        dst = dtb[pr][:, g0 * 128:(g0 + 16) * 128].rearrange(
                            "p (kt e) -> p kt e", kt=16)
                        e.dma_start(out=dst, in_=src).then_inc(sem, 16)
                t_dt = pb.dma("gpsimd", lddt, n=2, deps=[dt_ld[pr]])
                pw = e8 % 2

                def eomm(t, e8=e8, pr=pr, pw=pw):
                    last = None
                    for st_ in range(32):
                        last = t.matmul(out=ps3[pw][:, :C], lhsT=dtb[pr][:, st_ * 128:(st_ + 1) * 128],
                                        rhs=STe[st_][:], start=(st_ == 0), stop=(st_ == 31))
                    return last
                t_em = pb.comp("tensor", eomm, deps=[t_dt, tr3[0], st3[pw]])
                dt_ld[pr] = t_em

                def fin(v, e8=e8, pw=pw):
                    return v.scalar_tensor_tensor(out=outb3[pw][:], in0=ps3[pw][:, :C],
                                                  scalar=RW[3] * 0.3, in1=ACC3[e8][:],
                                                  op0=OP.mult, op1=OP.add)
                t_f = pb.comp("vector", fin, deps=[t_em, t_la3, dma_out3[pw]])
                st3[pw] = t_f

                def sto(e, sem, e8=e8, pw=pw, t0=t0):
                    e.dma_start(out=xout[e8 * 128:(e8 + 1) * 128, t0:t0 + C],
                                in_=outb3[pw][:]).then_inc(sem, 16)
                dma_out3[pw] = pb.dma("sync", sto, n=1, deps=[t_f])
            ste_use[0] = dt_ld[1]
            xbq_ld[0] = dt_ld[1]
            pb.barrier()
        pb.barrier()
        pb.es = old_es
    return pb.tok("sync")


def host_prep_l1_attn_a(inp):
    pre = "L1_attn_a_"
    col8 = lambda a: np.ascontiguousarray(np.asarray(a).reshape(-1, 128).T)
    r = {}
    r["ln_g8"] = col8(inp[pre + "ln_g"]); r["ln_b8"] = col8(inp[pre + "ln_b"])
    wxp = np.zeros((1024, 256), np.float32); wxp[:, :192] = inp[pre + "w_xproj"]
    r["w_xprojP"] = wxp
    r["w_dt"] = inp[pre + "w_dt"]
    r["b_dt8"] = col8(inp[pre + "b_dt"])
    r["A8"] = np.ascontiguousarray(
        inp[pre + "A_log"].reshape(8, 128, 64).transpose(1, 0, 2).reshape(128, 512))
    r["D8"] = col8(inp[pre + "D_ssm"])
    r["w_ssm_out"] = inp[pre + "w_ssm_out"]
    qorder = [0, 4, 1, 5, 2, 6, 3, 7, 8, 12, 9, 13, 10, 14, 11, 15]
    wq = inp[pre + "wq"]
    wo = inp[pre + "wo"]
    wqP = np.concatenate([wq[:, h * 64:(h + 1) * 64] for h in qorder], axis=1)
    woP = np.concatenate([wo[h * 64:(h + 1) * 64, :] for h in qorder], axis=0)
    r["wq"] = np.ascontiguousarray(wqP)
    r["wo"] = np.ascontiguousarray(woP)
    r["wk"] = inp[pre + "wk"]; r["wv"] = inp[pre + "wv"]
    r["w_wm"] = inp[pre + "w_wm"]; r["b_wm8"] = col8(inp[pre + "b_wm"])
    r["epiT"] = np.ascontiguousarray(inp[pre + "episodic"][0].T)
    r["w_dc1"] = inp[pre + "w_dc1"]
    r["bdc1_4"] = col8(inp[pre + "b_dc1"])
    r["w_dc2"] = inp[pre + "w_dc2"]
    r["bdc2_8"] = col8(inp[pre + "b_dc2"])
    r["wc32"] = np.ascontiguousarray(
        inp[pre + "w_conv"].reshape(8, 128, 4).transpose(1, 0, 2).reshape(128, 32))
    r["b_conv8"] = col8(inp[pre + "b_conv"])
    rb = np.asarray(inp[pre + "rel_bias"])
    i_ = np.arange(128)[:, None]
    j_ = np.arange(128)[None, :]
    blocks = []
    for rr in (-2, -1, 0, 1, 2):
        rel = 128 * rr + j_ - i_
        allowed = np.abs(rel) <= 256
        idx = np.clip(rel + 256, 0, 512)
        bt = rb[idx, :]
        mb = np.where(allowed[:, :, None], bt, np.float32(-1e10))
        blocks.append(mb)
    relbm = np.concatenate(blocks, axis=1)
    r["relbm"] = np.ascontiguousarray(relbm.transpose(2, 0, 1).reshape(2048, 640)).astype(np.float32)
    return r


L1A_WSPECS = [("ln_g8", [128, 8]), ("ln_b8", [128, 8]), ("w_xprojP", [1024, 256]),
              ("w_dt", [64, 1024]), ("b_dt8", [128, 8]), ("A8", [128, 512]),
              ("D8", [128, 8]), ("w_ssm_out", [1024, 1024]), ("wq", [1024, 1024]),
              ("wk", [1024, 256]), ("wv", [1024, 256]), ("wo", [1024, 1024]),
              ("w_wm", [1024, 1024]), ("b_wm8", [128, 8]), ("epiT", [128, 4096]),
              ("w_dc1", [128, 512]), ("bdc1_4", [128, 4]), ("w_dc2", [512, 1024]),
              ("bdc2_8", [128, 8]), ("wc32", [128, 32]), ("b_conv8", [128, 8]),
              ("relbm", [2048, 640])]

L1A_SCRATCH = [("xa", [1024, 4096], "f32"), ("xb_scr", [1024, 4096], "bf16"),
               ("acc", [1024, 4096], "f32"), ("decf", [1024, 4096], "bf16"),
               ("dectok", [4096, 1024], "bf16"), ("bcr", [128, 4096], "bf16")]



_CVALS = [PI / 2, 1e-5, 1e-24, 1.0]


def _build_stage(emit_fn, wspecs, scratch_specs, per_core_w=False):
    nc = bass.Bass("TRN2", target_bir_lowering=False)
    xin = nc.dram_tensor("xin", [1024, 4096], F32, kind="ExternalInput").ap()
    W = {name: nc.dram_tensor(name, list(shape), F32, kind="ExternalInput").ap()
         for name, shape in wspecs}
    xout = nc.dram_tensor("xout", [1024, 4096], F32, kind="ExternalOutput").ap()
    scr = {name: nc.dram_tensor("scr_" + name, list(shape),
                                BF16 if dt == "bf16" else F32, kind="Internal").ap()
           for name, shape, dt in scratch_specs}
    with ExitStack() as es:
        pb = PB(nc, es)
        cst = Consts(pb)
        cst.build(extra_cvals=_CVALS)
        emit_fn(pb, cst, W, xin, xout, scr)
        pb.finish()
        block = es.enter_context(nc.Block())
        pb.finalize(block)
    return nc


_STAGE_CACHE = {}


def _run_stage(name, emit_fn, wspecs, scratch_specs, hw, xs):
    if name not in _STAGE_CACHE:
        _STAGE_CACHE[name] = _build_stage(emit_fn, wspecs, scratch_specs)
    nc = _STAGE_CACHE[name]
    in_maps = []
    for r in range(NCORES):
        m = dict(hw[r] if isinstance(hw, list) else hw)
        m["xin"] = xs[r]
        in_maps.append(m)
    res = run_bass_kernel_spmd(nc, in_maps, core_ids=list(range(NCORES)))
    return [np.asarray(res.results[r]["xout"]) for r in range(NCORES)]


L0F_SCRATCH = [("xa", [1024, 4096], "f32")]


def _emit_l0aa(pb, cst, W, xin, xout, scr):
    emit_l0_attn_a(pb, cst, W, xin, xout)


def _emit_l0ab(pb, cst, W, xin, xout, scr):
    emit_l0_attn_b(pb, cst, W, xin, xout)


def _emit_l0f(pb, cst, W, xin, xout, scr):
    emit_l0_ffn(pb, cst, W, xin, xout, scr)


def _emit_l1a(pb, cst, W, xin, xout, scr):
    emit_l1_attn_a(pb, cst, W, xin, xout, scr)


def _emit_l1f(pb, cst, W, xin, xout, scr):
    emit_l1_ffn(pb, cst, W, xin, xout, scr, "f1_")


def _emit_l1b(pb, cst, W, xin, xout, scr):
    emit_l1_attn_b(pb, cst, W, xin, xout)


def kernel_unfused(**inputs):
    x0 = np.asarray(inputs["L0_attn_a_x"])
    xs = [np.ascontiguousarray(x0[b].T) for b in range(NCORES)]

    hw_aa = host_prep_l0_attn_a(inputs)
    hw_f = host_prep_l0_ffn(inputs)
    hw_ab = host_prep_l0_attn_b(inputs)
    hw_1a = host_prep_l1_attn_a(inputs)
    hw_1f = [host_prep_l1_ffn(inputs, r) for r in range(NCORES)]
    hw_1b = host_prep_l1_attn_b(inputs)

    xs = _run_stage("l0aa", _emit_l0aa, L0AA_WSPECS, [], hw_aa, xs)
    xs = _run_stage("l0f", _emit_l0f, L0F_WSPECS, L0F_SCRATCH, hw_f, xs)
    xs = _run_stage("l0ab", _emit_l0ab, L0AB_WSPECS, [], hw_ab, xs)
    xs = _run_stage("l0f", _emit_l0f, L0F_WSPECS, L0F_SCRATCH, hw_f, xs)
    xs = _run_stage("l1a", _emit_l1a, L1A_WSPECS, L1A_SCRATCH, hw_1a, xs)
    xs = _run_stage("l1f", _emit_l1f, L1F_WSPECS, L1F_SCRATCH, hw_1f, xs)
    xs = _run_stage("l1b", _emit_l1b, L1B_WSPECS, [], hw_1b, xs)
    xs = _run_stage("l1f", _emit_l1f, L1F_WSPECS, L1F_SCRATCH, hw_1f, xs)

    return np.stack([xs[b].T for b in range(NCORES)], axis=0)



def _w_prefix(wspecs, pre):
    return [(pre + n, s) for n, s in wspecs]


def build_fused():
    nc = bass.Bass("TRN2", target_bir_lowering=False)
    xin = nc.dram_tensor("xin", [1024, 4096], F32, kind="ExternalInput").ap()
    xout = nc.dram_tensor("xout", [1024, 4096], F32, kind="ExternalOutput").ap()

    def decl(wspecs, pre):
        return {n: nc.dram_tensor(pre + n, list(s), F32, kind="ExternalInput").ap()
                for n, s in wspecs}
    W_aa = decl(L0AA_WSPECS, "aa_")
    W_f = decl(L0F_WSPECS, "f_")
    W_ab = decl(L0AB_WSPECS, "ab_")
    W_1a = decl(L1A_WSPECS, "oa_")
    W_1f = decl(L1F_WSPECS, "of_")
    W_1b = decl(L1B_WSPECS, "ob_")

    xA = nc.dram_tensor("xA", [1024, 4096], F32, kind="Internal").ap()
    xB = nc.dram_tensor("xB", [1024, 4096], F32, kind="Internal").ap()

    def declscr(specs, pre):
        return {n: nc.dram_tensor(pre + n, list(s), BF16 if dt == "bf16" else F32,
                                  kind="Internal").ap()
                for n, s, dt in specs}
    scr_f = declscr(L0F_SCRATCH, "sf_")
    scr_1a = declscr(L1A_SCRATCH, "sa_")
    scr_1f = declscr(L1F_SCRATCH, "sm_")

    with ExitStack() as es:
        pb = PB(nc, es)
        cst = Consts(pb)
        cst.build(extra_cvals=_CVALS)

        stages = [
            (lambda p, c, i, o: emit_l0_attn_a(p, c, W_aa, i, o), xin, xA),
            (lambda p, c, i, o: emit_l0_ffn(p, c, W_f, i, o, scr_f, tag="l0fA_"), xA, xB),
            (lambda p, c, i, o: emit_l0_attn_b(p, c, W_ab, i, o), xB, xA),
            (lambda p, c, i, o: emit_l0_ffn(p, c, W_f, i, o, scr_f, tag="l0fB_"), xA, xB),
            (lambda p, c, i, o: emit_l1_attn_a(p, c, W_1a, i, o, scr_1a), xB, xA),
            (lambda p, c, i, o: emit_l1_ffn(p, c, W_1f, i, o, scr_1f, "m1_"), xA, xB),
            (lambda p, c, i, o: emit_l1_attn_b(p, c, W_1b, i, o), xB, xA),
            (lambda p, c, i, o: emit_l1_ffn(p, c, W_1f, i, o, scr_1f, "m2_"), xA, xout),
        ]
        for fn, src, dst in stages:
            with ExitStack() as es_i:
                old = pb.es
                pb.es = es_i
                fn(pb, cst, src, dst)
                pb.barrier()
                pb.es = old
        pb.finish()
        block = es.enter_context(nc.Block())
        pb.finalize(block)
    return nc


_FUSED_CACHE = {}


def kernel(**inputs):
    x0 = np.asarray(inputs["L0_attn_a_x"])
    hw_aa = host_prep_l0_attn_a(inputs)
    hw_f = host_prep_l0_ffn(inputs)
    hw_ab = host_prep_l0_attn_b(inputs)
    hw_1a = host_prep_l1_attn_a(inputs)
    hw_1b = host_prep_l1_attn_b(inputs)

    if "nc" not in _FUSED_CACHE:
        _FUSED_CACHE["nc"] = build_fused()
    nc = _FUSED_CACHE["nc"]

    in_maps = []
    for r in range(NCORES):
        m = {"xin": np.ascontiguousarray(x0[r].T)}
        for pre, hw in [("aa_", hw_aa), ("f_", hw_f), ("ab_", hw_ab),
                        ("oa_", hw_1a), ("ob_", hw_1b)]:
            for k, v in hw.items():
                m[pre + k] = v
        for k, v in host_prep_l1_ffn(inputs, r).items():
            m["of_" + k] = v
        in_maps.append(m)

    res = run_bass_kernel_spmd(nc, in_maps, core_ids=list(range(NCORES)))
    return np.stack([np.asarray(res.results[r]["xout"]).T for r in range(NCORES)], axis=0)
